# version 05
# baseline (speedup 1.0000x reference)
import sys
sys.path.insert(0, "/opt/trn_rl_repo")
import contextlib
import numpy as np
import ml_dtypes
import concourse.bass as bass
import concourse.mybir as mybir
from concourse.bass_utils import run_bass_kernel_spmd

F32 = mybir.dt.float32
BF16 = mybir.dt.bfloat16
I32 = mybir.dt.int32
U32 = mybir.dt.uint32
AF = mybir.ActivationFunctionType
ALU = mybir.AluOpType
AX = mybir.AxisListType
BF16NP = ml_dtypes.bfloat16

NCORES = 8
DEBUG_A = False
SKIP_PREP = False
PREP_MODE = 1
PREP_FENCE = False
DEBUG_C = False
BATCH, T, H = 2, 8192, 1024
DI = 2048
NST = 16
TPC = 2048
NTOK = BATCH * T
THALO = 2176

ENGS = ("tensor", "vector", "scalar", "gpsimd", "sync")


class Buf:
    __slots__ = ("h", "writer", "readers", "name", "_ctx", "t", "dsem", "dcount")

    def __init__(self, h, name):
        self.h = h
        self.name = name
        self.writer = None
        self.readers = {}
        self.dsem = None
        self.dcount = 0

    def __getitem__(self, sl):
        return self.h[sl]

    @property
    def ap(self):
        return self.h


class Prog:

    SAME_ENGINE_SYNC = ("vector", "scalar", "gpsimd")

    def __init__(self, name=""):
        self.nc = bass.Bass("TRN2", target_bir_lowering=False, debug=False)
        self.es = contextlib.ExitStack()
        self.ops = {e: [] for e in ENGS}
        self.sems = {}
        self.engsem = {}
        self._dsem_n = 0
        self._bregs = {}
        self._epoch = 0
        self._eng_epoch = {e: 0 for e in ENGS}
        for e in ENGS:
            self.engsem[e] = self._sem("S_" + e)

    def _sem(self, name):
        h = self.es.enter_context(self.nc.semaphore(name))
        self.sems[name] = [h, 0]
        return name

    def _agent_sem(self, agent):
        return agent[2:] if agent.startswith("D:") else self.engsem[agent]

    def sb(self, name, shape, dtype):
        ctx = self.nc.sbuf_tensor(name, shape, dtype)
        h = ctx.__enter__()
        b = Buf(h, name)
        b._ctx = ctx
        return b

    def free(self, *bufs):
        for b in bufs:
            b._ctx.__exit__(None, None, None)
        self._epoch += 1

    def ps(self, name, shape, dtype=F32):
        ctx = self.nc.psum_tensor(name, shape, dtype)
        h = ctx.__enter__()
        b = Buf(h, name)
        b._ctx = ctx
        return b

    def dram(self, name, shape, dtype, kind="Internal"):
        t = self.nc.dram_tensor(name, shape, dtype, kind=kind)
        b = Buf(t.ap(), name)
        b.t = t
        return b

    def _buf_dsem(self, b):
        if b.dsem is None:
            self._dsem_n += 1
            b.dsem = self._sem(f"D{self._dsem_n}_{b.name[:12]}")
        return b.dsem

    def fence(self, eng):
        for e in ENGS:
            if e != eng and self.sems[self.engsem[e]][1] > 0:
                self.raw_wait(eng, self.engsem[e], self.sems[self.engsem[e]][1])
        for name, (h, v) in list(self.sems.items()):
            if name.startswith("D") and "_" in name and v > 0:
                self.raw_wait(eng, name, v)

    def emit(self, eng, fn):
        self.ops[eng].append(fn)

    def raw_wait(self, eng, sem, val):
        if val <= 0:
            return
        h = self.sems[sem][0]
        self.emit(eng, lambda e, h=h, v=val: e.wait_ge(h, v))

    def _need_wait(self, me, other):
        if other.startswith("D:"):
            return True
        if me == other:
            return other in ("vector", "scalar", "gpsimd") or False
        return True

    def _hazard_waits(self, me, reads, writes, same_engine_raw=False):
        if self._eng_epoch[me] < self._epoch:
            self._eng_epoch[me] = self._epoch
            self.fence(me)
        need = {}

        def add(agent, val):
            if agent.startswith("D:"):
                sem = agent[2:]
            else:
                if agent == me and agent not in self.SAME_ENGINE_SYNC and not same_engine_raw:
                    return
                if agent == me and me == "tensor":
                    return
                sem = self.engsem[agent]
            need[sem] = max(need.get(sem, 0), val)

        for b in reads:
            if b.writer:
                add(*b.writer)
        for b in writes:
            if b.writer:
                add(*b.writer)
            for agent, val in b.readers.items():
                add(agent, val)
        for sem, v in need.items():
            self.raw_wait(me, sem, v)

    def _record(self, agent, v, reads, writes):
        for b in reads:
            b.readers[agent] = max(b.readers.get(agent, 0), v)
        for b in writes:
            b.writer = (agent, v)
            b.readers = {}

    def dma(self, q, dst, src, reads=(), writes=(), **kw):
        assert writes, "dma needs writes=[dest buffer]"
        dbuf = writes[0]
        sem = self._buf_dsem(dbuf)
        self._hazard_waits(q, reads, writes)
        h = self.sems[sem][0]
        self.sems[sem][1] += 16
        v = self.sems[sem][1]
        self.emit(q, lambda e, d=dst, s=src, kw=kw, h=h: e.dma_start(out=d, in_=s, **kw).then_inc(h, 16))
        self._record("D:" + sem, v, reads, writes)
        return sem, v

    def idma(self, q, dst, src, out_off, in_off, reads=(), writes=(), **kw):
        assert writes
        dbuf = writes[0]
        sem = self._buf_dsem(dbuf)
        self._hazard_waits(q, reads, writes)
        h = self.sems[sem][0]
        self.sems[sem][1] += 16
        v = self.sems[sem][1]
        bc = kw.pop("bounds_check", None)
        cell = None
        if isinstance(bc, int):
            key = (q, bc)
            if key not in self._bregs:
                c = [None]
                self._bregs[key] = c
                self.emit(q, lambda e, c=c, bcv=bc: c.__setitem__(0, e.to_reg(bcv)))
            cell = self._bregs[key]

        def fn(e, d=dst, s=src, oo=out_off, io=in_off, kw=kw, h=h, cell=cell, bc=bc):
            kk = dict(kw)
            if cell is not None:
                kk["bounds_check"] = cell[0]
            elif bc is not None:
                kk["bounds_check"] = bc
            e.indirect_dma_start(out=d, out_offset=oo, in_=s, in_offset=io, **kk).then_inc(h, 16)
        self.emit(q, fn)
        self._record("D:" + sem, v, reads, writes)
        return sem, v

    def op(self, eng, meth, *args, reads=(), writes=(), same_engine_raw=False, **kw):
        self._hazard_waits(eng, reads, writes, same_engine_raw)
        sem = self.engsem[eng]
        h = self.sems[sem][0]
        self.sems[sem][1] += 1
        v = self.sems[sem][1]

        def fn(e, meth=meth, args=args, kw=kw, h=h):
            getattr(e, meth)(*args, **kw).then_inc(h, 1)
        self.emit(eng, fn)
        self._record(eng, v, reads, writes)
        return v

    def val(self, eng):
        return self.sems[self.engsem[eng]][1]

    def finalize(self):
        for name, (h, v) in list(self.sems.items()):
            if name.startswith("D") and "_" in name and v > 0:
                self.raw_wait("sync", name, v)
        for e in ENGS:
            v = self.sems[self.engsem[e]][1]
            if v > 0 and e != "sync":
                self.raw_wait("sync", self.engsem[e], v)
        block = self.es.enter_context(self.nc.Block())
        for e in ENGS:
            lst = self.ops[e]
            if not lst:
                continue
            runner = _mk_runner(lst)
            getattr(block, e)(runner)
        self.es.close()
        return self.nc


def _mk_runner(lst):
    def run(eng):
        for fn in lst:
            fn(eng)
    return run


def bcast_ap(dram_buf, nrows, ncols, offset=0):
    return bass.AP(dram_buf.t, offset, [[0, nrows], [1, ncols]])


def softplus_emit(p, out_sb, in_buf, in_ap, scratch, rows, cols):
    p.op("scalar", "activation", scratch[0:rows, 0:cols], in_ap, AF.Abs,
         reads=[in_buf], writes=[scratch])
    p.op("scalar", "activation", scratch[0:rows, 0:cols], scratch[0:rows, 0:cols],
         AF.Exp, scale=-1.0, reads=[scratch], writes=[scratch])
    p.op("scalar", "activation", scratch[0:rows, 0:cols], scratch[0:rows, 0:cols],
         AF.Ln, bias=1.0, reads=[scratch], writes=[scratch])
    p.op("scalar", "activation", out_sb[0:rows, 0:cols], in_ap, AF.Relu,
         reads=[in_buf], writes=[out_sb])
    return p.op("vector", "tensor_add", out_sb[0:rows, 0:cols], out_sb[0:rows, 0:cols],
                scratch[0:rows, 0:cols], reads=[out_sb, scratch], writes=[out_sb])



def build_A():
    p = Prog("A")
    xa = p.dram("xa", [THALO, H], F32, kind="ExternalInput")
    w_in = p.dram("w_in", [H, 2 * DI], F32, kind="ExternalInput")
    w_xp = p.dram("w_xp", [DI, 2 * NST + DI], F32, kind="ExternalInput")
    normw = p.dram("normw", [1, H], F32, kind="ExternalInput")
    convw = p.dram("convw", [128, 64], F32, kind="ExternalInput")
    convb = p.dram("convb", [128, 16], F32, kind="ExternalInput")
    ped = p.dram("ped", [T, H], F32, kind="ExternalInput")
    peu = p.dram("peu", [T, H], F32, kind="ExternalInput")

    o_xssm = p.dram("o_xssm", [DI, TPC], BF16, kind="ExternalOutput")
    o_z = p.dram("o_z", [DI, TPC], BF16, kind="ExternalOutput")
    o_dt = p.dram("o_dt", [DI, TPC], F32, kind="ExternalOutput")
    o_dtx = p.dram("o_dtx", [DI, TPC], F32, kind="ExternalOutput")
    o_B = p.dram("o_B", [NST, TPC], F32, kind="ExternalOutput")
    o_C = p.dram("o_C", [NST, TPC], F32, kind="ExternalOutput")
    o_tbl = p.dram("o_tbl", [T, 2 * H], BF16 if PREP_MODE != 2 else F32, kind="ExternalOutput")
    h_hbm = p.dram("h_hbm", [THALO, H], BF16, kind="ExternalOutput" if DEBUG_A else "Internal")

    NT_LN = THALO // 128

    s_xs_bf = p.sb("s_xs_bf", [128, 16 * TPC], BF16)
    s_cw = p.sb("s_cw", [128, 64], F32)
    s_cb = p.sb("s_cb", [128, 16], F32)
    s_wbc = p.sb("s_wbc", [128, 16 * 32], F32)
    s_bc = p.sb("s_bc", [32, 512], F32)
    psA = [p.ps(f"psA{i}", [128, 512], F32) for i in range(2)]
    ps_bc4 = [p.ps(f"ps_bc{i}", [32, 512], F32) for i in range(4)]
    psB = [p.ps(f"psB{i}", [128, 512], F32) for i in range(2)]

    p.dma("gpsimd", s_cw[:, :], convw[:, :], writes=[s_cw])
    p.dma("gpsimd", s_cb[:, :], convb[:, :], writes=[s_cb])
    for c in range(16):
        p.dma("gpsimd", s_wbc[:, c * 32:(c + 1) * 32], w_xp[c * 128:(c + 1) * 128, 0:32],
              reads=[w_xp], writes=[s_wbc])

    sx = [p.sb(f"sx{i}", [128, H], F32) for i in range(2)]
    s_sq = p.sb("s_sq", [128, H], F32)
    s_hf = p.sb("s_hf", [128, H], F32)
    s_h = [p.sb(f"s_h{i}", [128, H], BF16) for i in range(2)]
    s_nw = p.sb("s_nw", [128, H], F32)
    cols = {k: p.sb("c_" + k, [128, 1], F32)
            for k in ["sums", "sumsq", "mean", "ex2", "var", "d1", "sq1", "r1",
                      "ratio", "d2", "sq2", "r2", "r12", "tmp"]}
    ln_group = [*sx, s_sq, s_hf, *s_h, s_nw] + list(cols.values())

    p.dma("gpsimd", s_nw[:, :], bcast_ap(normw, 128, H), writes=[s_nw])

    for i in range(NT_LN):
        buf = sx[i % 2]
        hb = s_h[i % 2]
        p.dma("gpsimd", buf[:, :], xa[i * 128:(i + 1) * 128, :], reads=[xa], writes=[buf])
        p.op("scalar", "activation", s_sq[:, :], buf[:, :], AF.Square,
             accum_out=cols["sumsq"][:, :], reads=[buf], writes=[s_sq, cols["sumsq"]])
        p.op("vector", "reduce_sum", cols["sums"][:, :], buf[:, :], AX.X,
             reads=[buf], writes=[cols["sums"]])
        p.op("vector", "tensor_scalar_mul", cols["mean"][:, :], cols["sums"][:, :], 1.0 / H,
             reads=[cols["sums"]], writes=[cols["mean"]])
        p.op("vector", "tensor_scalar_mul", cols["ex2"][:, :], cols["sumsq"][:, :], 1.0 / H,
             reads=[cols["sumsq"]], writes=[cols["ex2"]])
        p.op("vector", "tensor_mul", cols["tmp"][:, :], cols["mean"][:, :], cols["mean"][:, :],
             reads=[cols["mean"]], writes=[cols["tmp"]])
        p.op("vector", "tensor_sub", cols["var"][:, :], cols["ex2"][:, :], cols["tmp"][:, :],
             reads=[cols["ex2"], cols["tmp"]], writes=[cols["var"]])
        p.op("vector", "tensor_scalar_add", cols["d1"][:, :], cols["var"][:, :], 1e-5,
             reads=[cols["var"]], writes=[cols["d1"]])
        p.op("vector", "reciprocal", cols["tmp"][:, :], cols["d1"][:, :],
             reads=[cols["d1"]], writes=[cols["tmp"]])
        p.op("vector", "tensor_mul", cols["ratio"][:, :], cols["var"][:, :], cols["tmp"][:, :],
             reads=[cols["var"], cols["tmp"]], writes=[cols["ratio"]])
        p.op("vector", "tensor_scalar_add", cols["d2"][:, :], cols["ratio"][:, :], 1e-6,
             reads=[cols["ratio"]], writes=[cols["d2"]])
        p.op("scalar", "sqrt", cols["sq1"][:, :], cols["d1"][:, :],
             reads=[cols["d1"]], writes=[cols["sq1"]])
        p.op("scalar", "sqrt", cols["sq2"][:, :], cols["d2"][:, :],
             reads=[cols["d2"]], writes=[cols["sq2"]])
        p.op("vector", "reciprocal", cols["r1"][:, :], cols["sq1"][:, :],
             reads=[cols["sq1"]], writes=[cols["r1"]])
        p.op("vector", "reciprocal", cols["r2"][:, :], cols["sq2"][:, :],
             reads=[cols["sq2"]], writes=[cols["r2"]])
        p.op("vector", "tensor_mul", cols["r12"][:, :], cols["r1"][:, :], cols["r2"][:, :],
             reads=[cols["r1"], cols["r2"]], writes=[cols["r12"]])
        p.op("vector", "tensor_scalar", s_hf[:, :], buf[:, :], cols["mean"][:, :],
             cols["r12"][:, :], ALU.subtract, ALU.mult,
             reads=[buf, cols["mean"], cols["r12"]], writes=[s_hf])
        p.op("vector", "tensor_mul", hb[:, :], s_hf[:, :], s_nw[:, :],
             reads=[s_hf, s_nw], writes=[hb])
        p.dma("sync", h_hbm[i * 128:(i + 1) * 128, :], hb[:, :], reads=[hb], writes=[h_hbm])

    p.free(*reversed(ln_group))
    p.fence("gpsimd")
    p.fence("sync")

    s_ht = [p.sb(f"s_ht{c}", [128, THALO], BF16) for c in range(8)]
    s_win = [p.sb(f"s_win{c}", [128, 2 * DI], BF16) for c in range(8)]
    s_xzp = [p.sb(f"s_xzp{i}", [128, THALO], F32) for i in range(2)]
    s_zb = p.sb("s_zb", [128, THALO], BF16)
    s_acc = p.sb("s_acc", [128, TPC], F32)
    s_xs_f = p.sb("s_xs_f", [128, TPC], F32)
    ph2_group = [*s_ht, *s_win, *s_xzp, s_zb, s_acc, s_xs_f]

    for c in range(8):
        p.dma("gpsimd", s_win[c][:, :], w_in[c * 128:(c + 1) * 128, :],
              reads=[w_in], writes=[s_win[c]])
    for c in range(8):
        p.dma("sync", s_ht[c][:, :], h_hbm[:, c * 128:(c + 1) * 128],
              transpose=True, reads=[h_hbm], writes=[s_ht[c]])

    TCH = [(0, 512), (512, 512), (1024, 512), (1536, 512), (2048, 128)]
    for og in range(32):
        is_x = og < 16
        dg = og if is_x else og - 16
        xz_sb = s_xzp[og % 2]
        for (t0, tw) in TCH:
            pst = psA[(t0 // 512) % 2]
            for c in range(8):
                p.op("tensor", "matmul", pst[:, 0:tw],
                     s_win[c][:, og * 128:(og + 1) * 128],
                     s_ht[c][:, t0:t0 + tw],
                     start=(c == 0), stop=(c == 7),
                     reads=[s_win[c], s_ht[c]], writes=[pst])
            if is_x:
                p.op("vector", "tensor_copy", xz_sb[:, t0:t0 + tw], pst[:, 0:tw],
                     reads=[pst], writes=[xz_sb])
            else:
                p.op("scalar", "copy", s_zb[:, t0:t0 + tw], pst[:, 0:tw],
                     reads=[pst], writes=[s_zb])
        if is_x:
            p.op("vector", "tensor_scalar", s_acc[:, :], xz_sb[:, 0:TPC],
                 s_cw[:, dg * 4:dg * 4 + 1], None, ALU.mult,
                 reads=[xz_sb, s_cw], writes=[s_acc])
            for j in (1, 2, 3):
                p.op("vector", "scalar_tensor_tensor", s_acc[:, :],
                     xz_sb[:, j:j + TPC], s_cw[:, dg * 4 + j:dg * 4 + j + 1], s_acc[:, :],
                     ALU.mult, ALU.add, reads=[xz_sb, s_cw, s_acc], writes=[s_acc])
            p.op("scalar", "activation", s_xs_f[:, :], s_acc[:, :], AF.Silu,
                 bias=s_cb[:, dg:dg + 1], reads=[s_acc, s_cb], writes=[s_xs_f])
            p.op("vector", "tensor_copy", s_xs_bf[:, dg * TPC:(dg + 1) * TPC], s_xs_f[:, :],
                 reads=[s_xs_f], writes=[s_xs_bf])
            p.dma("sync", o_xssm[dg * 128:(dg + 1) * 128, :],
                  s_xs_bf[:, dg * TPC:(dg + 1) * TPC], reads=[s_xs_bf], writes=[o_xssm])
            for ti in range(4):
                p.op("tensor", "matmul", ps_bc4[ti][:, :],
                     s_wbc[:, dg * 32:(dg + 1) * 32],
                     s_xs_f[:, ti * 512:(ti + 1) * 512],
                     start=(dg == 0), stop=(dg == 15),
                     reads=[s_wbc, s_xs_f], writes=[ps_bc4[ti]])
            if dg == 15:
                for ti in range(4):
                    p.op("vector", "tensor_copy", s_bc[:, :], ps_bc4[ti][:, :],
                         reads=[ps_bc4[ti]], writes=[s_bc])
                    p.dma("sync", o_B[:, ti * 512:(ti + 1) * 512], s_bc[0:16, :],
                          reads=[s_bc], writes=[o_B])
                    p.dma("sync", o_C[:, ti * 512:(ti + 1) * 512], s_bc[16:32, :],
                          reads=[s_bc], writes=[o_C])
        else:
            p.dma("sync", o_z[dg * 128:(dg + 1) * 128, :], s_zb[:, 3:3 + TPC],
                  reads=[s_zb], writes=[o_z])

    p.free(*reversed(ph2_group))
    p.fence("gpsimd")

    s_wxp = [p.sb(f"s_wxp{c}", [128, 2 * NST + DI], BF16) for c in range(16)]
    s_dt = p.sb("s_dt", [128, 512], F32)
    s_sp = p.sb("s_sp", [128, 512], F32)
    s_dtx = p.sb("s_dtx", [128, 512], F32)

    for c in range(16):
        p.dma("gpsimd", s_wxp[c][:, :], w_xp[c * 128:(c + 1) * 128, :],
              reads=[w_xp], writes=[s_wxp[c]])

    TCH2 = [(0, 512), (512, 512), (1024, 512), (1536, 512)]
    for dgo in range(16):
        for (t0, tw) in TCH2:
            pst = psB[(t0 // 512) % 2]
            for c in range(16):
                p.op("tensor", "matmul", pst[:, 0:tw],
                     s_wxp[c][:, 32 + dgo * 128:32 + (dgo + 1) * 128],
                     s_xs_bf[:, c * TPC + t0:c * TPC + t0 + tw],
                     start=(c == 0), stop=(c == 15),
                     reads=[s_wxp[c], s_xs_bf], writes=[pst])
            softplus_emit(p, s_dt, pst, pst[:, 0:tw], s_sp, 128, tw)
            p.op("vector", "tensor_mul", s_dtx[:, 0:tw], s_dt[:, 0:tw],
                 s_xs_bf[:, dgo * TPC + t0:dgo * TPC + t0 + tw],
                 reads=[s_dt, s_xs_bf], writes=[s_dtx])
            p.dma("sync", o_dt[dgo * 128:(dgo + 1) * 128, t0:t0 + tw], s_dt[:, 0:tw],
                  reads=[s_dt], writes=[o_dt])
            p.dma("sync", o_dtx[dgo * 128:(dgo + 1) * 128, t0:t0 + tw], s_dtx[:, 0:tw],
                  reads=[s_dtx], writes=[o_dtx])

    if PREP_FENCE:
        p.fence("gpsimd")
        p.fence("sync")
    s_pk = [p.sb(f"s_pk{i}", [128, H], BF16 if PREP_MODE != 2 else F32) for i in range(4)]
    for half, src in (((0, ped), (1, peu)) if not SKIP_PREP else ()):
        for i in range(T // 128):
            buf = s_pk[i % 4]
            p.dma("gpsimd", buf[:, :], src[i * 128:(i + 1) * 128, :],
                  reads=[src], writes=[buf])
            p.dma("sync", o_tbl[i * 128:(i + 1) * 128, half * H:(half + 1) * H],
                  buf[:, :], reads=[buf], writes=[o_tbl])

    return p.finalize()



_NC_CACHE = {}


def _get_nc(name, builder):
    if name not in _NC_CACHE:
        _NC_CACHE[name] = builder()
    return _NC_CACHE[name]


def run_A(inputs):
    x = np.asarray(inputs["L0_attn_x"], np.float32).reshape(NTOK, H)
    w_in = np.ascontiguousarray(inputs["L0_attn_w_in"], dtype=np.float32)
    w_xp = np.ascontiguousarray(inputs["L0_attn_w_xproj"], dtype=np.float32)
    normw = np.asarray(inputs["L0_attn_norm_w"], np.float32).reshape(1, H)
    convw = np.ascontiguousarray(
        np.asarray(inputs["L0_attn_conv_w"], np.float32).reshape(DI, 4)
        .reshape(16, 128, 4).transpose(1, 0, 2).reshape(128, 64))
    convb = np.ascontiguousarray(
        np.asarray(inputs["L0_attn_conv_b"], np.float32)
        .reshape(16, 128).transpose(1, 0))
    ed = np.asarray(inputs["L1_ffn_expert_down"], np.float32)
    eu = np.asarray(inputs["L1_ffn_expert_up"], np.float32)

    in_maps = []
    for c in range(NCORES):
        t0 = c * TPC
        xa = np.zeros((THALO, H), np.float32)
        lo = max(t0 - 3, (c // 4) * T)
        n_halo = t0 - lo
        xa[3 - n_halo:3 + TPC] = x[lo:t0 + TPC]
        in_maps.append(dict(
            xa=xa, w_in=w_in, w_xp=w_xp, normw=normw, convw=convw, convb=convb,
            ped=ed[c * T:(c + 1) * T], peu=eu[c * T:(c + 1) * T]))
    nc = _get_nc("A", build_A)
    res = run_bass_kernel_spmd(nc, in_maps, core_ids=list(range(NCORES)))
    r = res.results
    out = {}
    out["xssm"] = np.concatenate([r[c]["o_xssm"] for c in range(NCORES)], axis=1)
    out["z"] = np.concatenate([r[c]["o_z"] for c in range(NCORES)], axis=1)
    out["dt"] = np.concatenate([r[c]["o_dt"] for c in range(NCORES)], axis=1)
    out["dtx"] = np.concatenate([r[c]["o_dtx"] for c in range(NCORES)], axis=1)
    out["B"] = np.concatenate([r[c]["o_B"] for c in range(NCORES)], axis=1)
    out["C"] = np.concatenate([r[c]["o_C"] for c in range(NCORES)], axis=1)
    out["tbl"] = np.concatenate([r[c]["o_tbl"] for c in range(NCORES)], axis=0)
    if DEBUG_A:
        out["h"] = [r[c]["h_hbm"] for c in range(NCORES)]
    return out




def build_B():
    p = Prog("B")
    dts = p.dram("dts", [256, NTOK], F32, kind="ExternalInput")
    dtxs = p.dram("dtxs", [256, NTOK], F32, kind="ExternalInput")
    Bf = p.dram("Bf", [NST, NTOK], F32, kind="ExternalInput")
    Cf = p.dram("Cf", [NST, NTOK], F32, kind="ExternalInput")
    alog = p.dram("alog", [128, 32], F32, kind="ExternalInput")
    o_y = p.dram("o_y", [256, NTOK], F32, kind="ExternalOutput")

    s_alog = p.sb("s_alog", [128, 32], F32)
    s_A = p.sb("s_A", [128, 32], F32)
    s_ones = p.sb("s_ones", [1, 128], F32)
    s_id = p.sb("s_id", [128, 128], BF16)
    s_dt = [p.sb(f"s_dt{i}", [128, 512], F32) for i in range(2)]
    s_dtx = [p.sb(f"s_dtx{i}", [128, 512], F32) for i in range(2)]
    s_bc = [p.sb(f"s_bc{i}", [16, 512], F32) for i in range(2)]
    s_cc = [p.sb(f"s_cc{i}", [16, 512], F32) for i in range(2)]
    s_sel = p.sb("s_sel", [16, 16 * 128], F32)
    s_h = [p.sb(f"s_h{n}", [128, 1024], F32) for n in range(NST)]
    s_dA = [p.sb(f"s_dA{i}", [128, 512], F32) for i in range(2)]
    s_u = [p.sb(f"s_u{i}", [128, 512], F32) for i in range(2)]
    s_hc = [p.sb(f"s_hc{i}", [128, 512], BF16) for i in range(2)]
    s_y = p.sb("s_y", [128, 512], F32)
    ps_b = [p.ps(f"ps_b{i}", [128, 512], F32) for i in range(2)]
    ps_c = [p.ps(f"ps_c{i}", [128, 512], F32) for i in range(2)]
    ps_y = [p.ps(f"ps_y{i}", [128, 512], F32) for i in range(2)]

    p.dma("gpsimd", s_alog[:, :], alog[:, :], reads=[alog], writes=[s_alog])
    p.op("scalar", "activation", s_A[:, :], s_alog[:, :], AF.Exp,
         reads=[s_alog], writes=[s_A])
    p.op("vector", "tensor_scalar_mul", s_A[:, :], s_A[:, :], -1.0,
         reads=[s_A], writes=[s_A])
    p.op("gpsimd", "memset", s_ones[:, :], 1.0, writes=[s_ones])
    p.op("gpsimd", "memset", s_id[:, :], 1.0, writes=[s_id])
    p.op("gpsimd", "affine_select", s_id[:, :], s_id[:, :], [[1, 128]],
         ALU.is_equal, 0.0, base=0, channel_multiplier=-1,
         reads=[s_id], writes=[s_id])
    p.op("gpsimd", "memset", s_sel[:, :], 1.0, writes=[s_sel])
    p.op("gpsimd", "affine_select",
         bass.AP(s_sel.h, 0, [[16 * 128, 16], [128, 16], [1, 128]]),
         bass.AP(s_sel.h, 0, [[16 * 128, 16], [128, 16], [1, 128]]),
         [[1, 16], [0, 128]], ALU.is_equal, 0.0, base=0, channel_multiplier=-1,
         reads=[s_sel], writes=[s_sel])

    NCH = NTOK // 512
    for blk in range(2):
        for ch in range(NCH):
            t0 = ch * 512
            par = ch % 2
            dt_b, dtx_b, bc_b, cc_b = s_dt[par], s_dtx[par], s_bc[par], s_cc[par]
            p.dma("gpsimd", dt_b[:, :], dts[blk * 128:(blk + 1) * 128, t0:t0 + 512],
                  reads=[dts], writes=[dt_b])
            p.dma("gpsimd", dtx_b[:, :], dtxs[blk * 128:(blk + 1) * 128, t0:t0 + 512],
                  reads=[dtxs], writes=[dtx_b])
            p.dma("scalar", bc_b[:, :], Bf[:, t0:t0 + 512], reads=[Bf], writes=[bc_b])
            p.dma("scalar", cc_b[:, :], Cf[:, t0:t0 + 512], reads=[Cf], writes=[cc_b])
            ypst = ps_y[par]
            for n in range(NST):
                psb = ps_b[n % 2]
                psc = ps_c[n % 2]
                sdA = s_dA[n % 2]
                su = s_u[n % 2]
                shc = s_hc[n % 2]
                half = ch % 2
                hbuf = s_h[n]
                p.op("tensor", "matmul", psb[:, :],
                     s_sel[:, n * 128:(n + 1) * 128], bc_b[:, :],
                     start=True, stop=True, reads=[s_sel, bc_b], writes=[psb])
                p.op("scalar", "activation", sdA[:, :], dt_b[:, :], AF.Exp,
                     scale=s_A[:, blk * 16 + n:blk * 16 + n + 1],
                     reads=[dt_b, s_A], writes=[sdA])
                p.op("vector", "tensor_mul", su[:, :], dtx_b[:, :], psb[:, :],
                     reads=[dtx_b, psb], writes=[su])
                init = 0.0 if (ch % 16) == 0 else hbuf[:, (1 - half) * 512 + 511:(1 - half) * 512 + 512]
                p.op("vector", "tensor_tensor_scan",
                     hbuf[:, half * 512:(half + 1) * 512], sdA[:, :], su[:, :],
                     init, ALU.mult, ALU.add,
                     reads=[sdA, su, hbuf], writes=[hbuf], same_engine_raw=True)
                p.op("tensor", "matmul", psc[:, :],
                     s_sel[:, n * 128:(n + 1) * 128], cc_b[:, :],
                     start=True, stop=True, reads=[s_sel, cc_b], writes=[psc])
                p.op("vector", "tensor_mul", shc[:, :],
                     hbuf[:, half * 512:(half + 1) * 512], psc[:, :],
                     reads=[hbuf, psc], writes=[shc])
                p.op("tensor", "matmul", ypst[:, :], s_id[:, :], shc[:, :],
                     start=(n == 0), stop=(n == NST - 1),
                     reads=[s_id, shc], writes=[ypst])
            p.op("scalar", "copy", s_y[:, :], ypst[:, :], reads=[ypst], writes=[s_y])
            p.dma("sync", o_y[blk * 128:(blk + 1) * 128, t0:t0 + 512], s_y[:, :],
                  reads=[s_y], writes=[o_y])

    return p.finalize()


def run_B(inputs, a_out):
    alog_full = np.asarray(inputs["L0_attn_A_log"], np.float32)
    dt = a_out["dt"]
    dtx = a_out["dtx"]
    Bf = np.ascontiguousarray(a_out["B"])
    Cf = np.ascontiguousarray(a_out["C"])
    in_maps = []
    for c in range(NCORES):
        d0 = c * 256
        al = alog_full[d0:d0 + 256].reshape(2, 128, 16).transpose(1, 0, 2).reshape(128, 32)
        in_maps.append(dict(
            dts=np.ascontiguousarray(dt[d0:d0 + 256]),
            dtxs=np.ascontiguousarray(dtx[d0:d0 + 256]),
            Bf=Bf, Cf=Cf, alog=np.ascontiguousarray(al)))
    nc = _get_nc("B", build_B)
    res = run_bass_kernel_spmd(nc, in_maps, core_ids=list(range(NCORES)))
    y = np.concatenate([res.results[c]["o_y"] for c in range(NCORES)], axis=0)
    return y




def build_C():
    p = Prog("C")
    x0 = p.dram("x0", [TPC, H], F32, kind="ExternalInput")
    yin = p.dram("yin", [DI, TPC], F32, kind="ExternalInput")
    xssm = p.dram("xssm", [DI, TPC], BF16, kind="ExternalInput")
    zin = p.dram("zin", [DI, TPC], BF16, kind="ExternalInput")
    dpar = p.dram("dpar", [128, 16], F32, kind="ExternalInput")
    w_out = p.dram("w_out", [DI, H], F32, kind="ExternalInput")
    gwT = p.dram("gwT", [H, 32], F32, kind="ExternalInput")
    sgu = p.dram("sgu", [H, H], F32, kind="ExternalInput")
    sdw = p.dram("sdw", [H // 2, H], F32, kind="ExternalInput")

    o_x2p = p.dram("o_x2p", [TPC, H], F32, kind="ExternalOutput")
    o_ln1 = p.dram("o_ln1", [TPC, H], BF16, kind="ExternalOutput")
    o_tke = p.dram("o_tke", [TPC, 4], F32, kind="ExternalOutput")
    o_gates = p.dram("o_gates", [TPC, 4], F32, kind="ExternalOutput")
    o_rank = p.dram("o_rank", [1, 4 * TPC], F32, kind="ExternalOutput")
    o_counts = p.dram("o_counts", [32, 1], F32, kind="ExternalOutput")
    tke_hbm = p.dram("tke_hbm", [TPC, 4], F32)
    shr_hbm = p.dram("shr_hbm", [TPC, H], BF16)
    o_x1dbg = p.dram("o_x1dbg", [TPC, H], F32, kind="ExternalOutput") if DEBUG_C else None
    o_mamdbg = p.dram("o_mamdbg", [128, 8 * TPC], BF16, kind="ExternalOutput") if DEBUG_C else None

    NT = TPC // 128

    s_wo = [p.sb(f"s_wo{c}", [128, H], BF16) for c in range(16)]
    s_gw = p.sb("s_gw", [128, 8 * 32], F32)
    s_sgu = [p.sb(f"s_sgu{c}", [128, H], BF16) for c in range(8)]
    s_sdw = [p.sb(f"s_sdw{c}", [128, H], BF16) for c in range(4)]
    s_dp = p.sb("s_dp", [128, 16], F32)
    s_id = p.sb("s_id", [128, 128], BF16)
    s_idf = p.sb("s_idf", [128, 128], F32)
    s_mam = p.sb("s_mam", [128, 8 * TPC], BF16)
    psW = [p.ps(f"psW{i}", [128, 512], F32) for i in range(2)]
    psT = p.ps("psT", [128, 128], BF16)
    psTf = p.ps("psTf", [128, 128], F32)
    psL = p.ps("psL", [128, 32], F32)
    psR = p.ps("psR", [1, 512], F32)

    for c in range(16):
        p.dma("gpsimd", s_wo[c][:, :], w_out[c * 128:(c + 1) * 128, :],
              reads=[w_out], writes=[s_wo[c]])
    for c in range(8):
        p.dma("gpsimd", s_gw[:, c * 32:(c + 1) * 32], gwT[c * 128:(c + 1) * 128, :],
              reads=[gwT], writes=[s_gw])
        p.dma("gpsimd", s_sgu[c][:, :], sgu[c * 128:(c + 1) * 128, :],
              reads=[sgu], writes=[s_sgu[c]])
    for c in range(4):
        p.dma("gpsimd", s_sdw[c][:, :], sdw[c * 128:(c + 1) * 128, :],
              reads=[sdw], writes=[s_sdw[c]])
    p.dma("gpsimd", s_dp[:, :], dpar[:, :], reads=[dpar], writes=[s_dp])
    p.op("gpsimd", "memset", s_id[:, :], 1.0, writes=[s_id])
    p.op("gpsimd", "affine_select", s_id[:, :], s_id[:, :], [[1, 128]],
         ALU.is_equal, 0.0, base=0, channel_multiplier=-1, reads=[s_id], writes=[s_id])
    p.op("gpsimd", "memset", s_idf[:, :], 1.0, writes=[s_idf])
    p.op("gpsimd", "affine_select", s_idf[:, :], s_idf[:, :], [[1, 128]],
         ALU.is_equal, 0.0, base=0, channel_multiplier=-1, reads=[s_idf], writes=[s_idf])

    s_y = [p.sb(f"s_yy{i}", [128, TPC], F32) for i in range(2)]
    s_xsb = [p.sb(f"s_xsb{i}", [128, TPC], BF16) for i in range(2)]
    s_zb2 = [p.sb(f"s_zb2{i}", [128, TPC], BF16) for i in range(2)]
    s_zs = p.sb("s_zs", [128, TPC], F32)
    s_t1 = p.sb("s_t1", [128, TPC], F32)
    s_y2 = [p.sb(f"s_y2_{i}", [128, TPC], BF16) for i in range(16)]

    for dg in range(16):
        yb, xb, zb = s_y[dg % 2], s_xsb[dg % 2], s_zb2[dg % 2]
        p.dma("gpsimd", yb[:, :], yin[dg * 128:(dg + 1) * 128, :], reads=[yin], writes=[yb])
        p.dma("gpsimd", xb[:, :], xssm[dg * 128:(dg + 1) * 128, :], reads=[xssm], writes=[xb])
        p.dma("gpsimd", zb[:, :], zin[dg * 128:(dg + 1) * 128, :], reads=[zin], writes=[zb])
        p.op("vector", "scalar_tensor_tensor", s_t1[:, :], xb[:, :],
             s_dp[:, dg:dg + 1], yb[:, :], ALU.mult, ALU.add,
             reads=[xb, s_dp, yb], writes=[s_t1])
        p.op("scalar", "activation", s_zs[:, :], zb[:, :], AF.Silu,
             reads=[zb], writes=[s_zs])
        p.op("vector", "tensor_mul", s_y2[dg][:, :], s_t1[:, :], s_zs[:, :],
             reads=[s_t1, s_zs], writes=[s_y2[dg]])
    for oc in range(8):
        for ti in range(4):
            pst = psW[ti % 2]
            for dg in range(16):
                p.op("tensor", "matmul", pst[:, :],
                     s_wo[dg][:, oc * 128:(oc + 1) * 128],
                     s_y2[dg][:, ti * 512:(ti + 1) * 512],
                     start=(dg == 0), stop=(dg == 15),
                     reads=[s_wo[dg], s_y2[dg]], writes=[pst])
            p.op("scalar", "copy", s_mam[:, oc * TPC + ti * 512:oc * TPC + (ti + 1) * 512],
                 pst[:, :], reads=[pst], writes=[s_mam])

    p.free(*reversed(s_y2))
    p.free(s_t1, s_zs, s_zb2[1], s_zb2[0], s_xsb[1], s_xsb[0], s_y[1], s_y[0])
    p.fence("gpsimd")
    p.fence("sync")

    if DEBUG_C:
        p.dma("sync", o_mamdbg[:, :], s_mam[:, :], reads=[s_mam], writes=[o_mamdbg])
    s_x0 = [p.sb(f"s_x0_{i}", [128, H], F32) for i in range(2)]
    s_x1 = p.sb("s_x1", [128, H], F32)
    s_sq2 = p.sb("s_sq2", [128, H], F32)
    s_ln1 = p.sb("s_ln1", [128, H], F32)
    s_ln1b = p.sb("s_ln1b", [128, H], BF16)
    s_mt = p.sb("s_mt", [128, H], BF16)
    s_l1tf = p.sb("s_l1tf", [128, 4 * H], F32)
    s_l1tb = p.sb("s_l1tb", [128, 4 * H], BF16)
    s_pun = p.sb("s_pun", [128, 32], F32)
    s_gv = p.sb("s_gv", [128, 8], F32)
    s_gates = p.sb("s_gates", [128, 4], F32)
    s_ei = p.sb("s_ei", [128, 4], F32)
    s_io32 = p.sb("s_io32", [128, 32], F32)
    s_scr32 = p.sb("s_scr32", [128, 32], F32)
    cols = {k: p.sb("c2_" + k, [128, 1], F32)
            for k in ["sums", "sumsq", "mean", "ex2", "var", "d1", "sq1", "r1",
                      "m", "negm", "wsum", "rw"]}
    s_act = [p.sb(f"s_act{i}", [128, 512], BF16) for i in range(4)]
    s_sg = p.sb("s_sg", [128, 512], F32)
    s_shT = p.sb("s_shT", [128, 8 * 512], BF16)
    psS = [p.ps(f"psS{i}", [128, 512], F32) for i in range(2)]

    p.op("gpsimd", "iota", s_io32[:, :], [[1, 32]], base=1, channel_multiplier=0,
         allow_small_or_imprecise_dtypes=True, writes=[s_io32])

    def emit_ln(p, xbuf, outbuf, cols, sq, rms_eps=None):
        p.op("scalar", "activation", sq[:, :], xbuf[:, :], AF.Square,
             accum_out=cols["sumsq"][:, :], reads=[xbuf], writes=[sq, cols["sumsq"]])
        p.op("vector", "reduce_sum", cols["sums"][:, :], xbuf[:, :], AX.X,
             reads=[xbuf], writes=[cols["sums"]])
        p.op("vector", "tensor_scalar_mul", cols["mean"][:, :], cols["sums"][:, :], 1.0 / H,
             reads=[cols["sums"]], writes=[cols["mean"]])
        p.op("vector", "tensor_scalar_mul", cols["ex2"][:, :], cols["sumsq"][:, :], 1.0 / H,
             reads=[cols["sumsq"]], writes=[cols["ex2"]])
        p.op("vector", "tensor_mul", cols["var"][:, :], cols["mean"][:, :], cols["mean"][:, :],
             reads=[cols["mean"]], writes=[cols["var"]])
        p.op("vector", "tensor_sub", cols["var"][:, :], cols["ex2"][:, :], cols["var"][:, :],
             reads=[cols["ex2"], cols["var"]], writes=[cols["var"]])
        p.op("vector", "tensor_scalar_add", cols["d1"][:, :], cols["var"][:, :], 1e-5,
             reads=[cols["var"]], writes=[cols["d1"]])
        p.op("scalar", "sqrt", cols["sq1"][:, :], cols["d1"][:, :],
             reads=[cols["d1"]], writes=[cols["sq1"]])
        p.op("vector", "reciprocal", cols["r1"][:, :], cols["sq1"][:, :],
             reads=[cols["sq1"]], writes=[cols["r1"]])
        p.op("vector", "tensor_scalar", outbuf[:, :], xbuf[:, :], cols["mean"][:, :],
             cols["r1"][:, :], ALU.subtract, ALU.mult,
             reads=[xbuf, cols["mean"], cols["r1"]], writes=[outbuf])

    for grp in range(4):
        for sub in range(4):
            i = grp * 4 + sub
            xb = s_x0[i % 2]
            p.dma("gpsimd", xb[:, :], x0[i * 128:(i + 1) * 128, :], reads=[x0], writes=[xb])
            emit_ln(p, xb, s_x1, cols, s_sq2)
            p.op("vector", "tensor_add", s_x1[:, :], s_x1[:, :], xb[:, :],
                 reads=[s_x1, xb], writes=[s_x1])
            for oc in range(8):
                p.op("tensor", "transpose", psT[:, :],
                     s_mam[:, oc * TPC + i * 128:oc * TPC + (i + 1) * 128], s_id[:, :],
                     reads=[s_mam, s_id], writes=[psT])
                p.op("vector", "tensor_copy", s_mt[:, oc * 128:(oc + 1) * 128], psT[:, :],
                     reads=[psT], writes=[s_mt])
            p.op("vector", "tensor_add", s_x1[:, :], s_x1[:, :], s_mt[:, :],
                 reads=[s_x1, s_mt], writes=[s_x1])
            p.dma("sync", o_x2p[i * 128:(i + 1) * 128, :], s_x1[:, :],
                  reads=[s_x1], writes=[o_x2p])
            emit_ln(p, s_x1, s_ln1, cols, s_sq2)
            p.op("vector", "tensor_copy", s_ln1b[:, :], s_ln1[:, :],
                 reads=[s_ln1], writes=[s_ln1b])
            p.dma("sync", o_ln1[i * 128:(i + 1) * 128, :], s_ln1b[:, :],
                  reads=[s_ln1b], writes=[o_ln1])
            for c in range(8):
                p.op("tensor", "transpose", psTf[:, :],
                     s_ln1[:, c * 128:(c + 1) * 128], s_idf[:, :],
                     reads=[s_ln1, s_idf], writes=[psTf])
                p.op("vector", "tensor_copy",
                     s_l1tf[:, c * 512 + sub * 128:c * 512 + (sub + 1) * 128],
                     psTf[:, :], reads=[psTf], writes=[s_l1tf])
                p.op("vector", "tensor_copy",
                     s_l1tb[:, c * 512 + sub * 128:c * 512 + (sub + 1) * 128],
                     psTf[:, :], reads=[psTf], writes=[s_l1tb])
            for c in range(8):
                p.op("tensor", "matmul", psL[:, :],
                     s_l1tf[:, c * 512 + sub * 128:c * 512 + (sub + 1) * 128],
                     s_gw[:, c * 32:(c + 1) * 32],
                     start=(c == 0), stop=(c == 7),
                     reads=[s_l1tf, s_gw], writes=[psL])
            p.op("vector", "reduce_max", cols["m"][:, :], psL[:, :], AX.X,
                 reads=[psL], writes=[cols["m"]])
            p.op("vector", "tensor_scalar_mul", cols["negm"][:, :], cols["m"][:, :], -1.0,
                 reads=[cols["m"]], writes=[cols["negm"]])
            p.op("scalar", "activation", s_pun[:, :], psL[:, :], AF.Exp,
                 bias=cols["negm"][:, :], reads=[psL, cols["negm"]], writes=[s_pun])
            p.op("vector", "max", s_gv[:, :], s_pun[:, :], reads=[s_pun], writes=[s_gv])
            p.op("vector", "reduce_sum", cols["wsum"][:, :], s_gv[:, 0:4], AX.X,
                 reads=[s_gv], writes=[cols["wsum"]])
            p.op("vector", "reciprocal", cols["rw"][:, :], cols["wsum"][:, :],
                 reads=[cols["wsum"]], writes=[cols["rw"]])
            p.op("vector", "tensor_scalar", s_gates[:, :], s_gv[:, 0:4],
                 cols["rw"][:, :], None, ALU.mult,
                 reads=[s_gv, cols["rw"]], writes=[s_gates])
            for j in range(4):
                p.op("vector", "scalar_tensor_tensor", s_scr32[:, :], s_pun[:, :],
                     s_gv[:, j:j + 1], s_io32[:, :], ALU.is_equal, ALU.mult,
                     accum_out=s_ei[:, j:j + 1],
                     reads=[s_pun, s_gv, s_io32], writes=[s_scr32, s_ei])
            p.op("vector", "tensor_scalar_add", s_ei[:, :], s_ei[:, :], -1.0,
                 reads=[s_ei], writes=[s_ei])
            p.dma("sync", tke_hbm[i * 128:(i + 1) * 128, :], s_ei[:, :],
                  reads=[s_ei], writes=[tke_hbm])
            p.dma("sync", o_tke[i * 128:(i + 1) * 128, :], s_ei[:, :],
                  reads=[s_ei], writes=[o_tke])
            p.dma("sync", o_gates[i * 128:(i + 1) * 128, :], s_gates[:, :],
                  reads=[s_gates], writes=[o_gates])
        for oc in range(8):
            pst = psS[oc % 2]
            for c in range(8):
                p.op("tensor", "matmul", pst[:, :],
                     s_sgu[c][:, oc * 128:(oc + 1) * 128],
                     s_l1tb[:, c * 512:(c + 1) * 512],
                     start=(c == 0), stop=(c == 7),
                     reads=[s_sgu[c], s_l1tb], writes=[pst])
            if oc < 4:
                p.op("scalar", "activation", s_sg[:, :], pst[:, :], AF.Silu,
                     reads=[pst], writes=[s_sg])
                p.op("vector", "tensor_copy", s_act[oc][:, :], s_sg[:, :],
                     reads=[s_sg], writes=[s_act[oc]])
            else:
                p.op("vector", "tensor_mul", s_act[oc - 4][:, :], s_act[oc - 4][:, :],
                     pst[:, :], reads=[s_act[oc - 4], pst], writes=[s_act[oc - 4]])
        for oc in range(8):
            pst = psS[oc % 2]
            for ic in range(4):
                p.op("tensor", "matmul", pst[:, :],
                     s_sdw[ic][:, oc * 128:(oc + 1) * 128],
                     s_act[ic][:, :],
                     start=(ic == 0), stop=(ic == 3),
                     reads=[s_sdw[ic], s_act[ic]], writes=[pst])
            p.op("scalar", "copy", s_shT[:, oc * 512:(oc + 1) * 512], pst[:, :],
                 reads=[pst], writes=[s_shT])
        for sub in range(4):
            i = grp * 4 + sub
            for oc in range(8):
                p.op("tensor", "transpose", psT[:, :],
                     s_shT[:, oc * 512 + sub * 128:oc * 512 + (sub + 1) * 128],
                     s_id[:, :], reads=[s_shT, s_id], writes=[psT])
                p.op("vector", "tensor_copy", s_mt[:, oc * 128:(oc + 1) * 128],
                     psT[:, :], reads=[psT], writes=[s_mt])
            p.dma("sync", shr_hbm[i * 128:(i + 1) * 128, :], s_mt[:, :],
                  reads=[s_mt], writes=[shr_hbm])

    s_a = [p.sb(f"s_a{i}", [128, H], F32) for i in range(2)]
    s_b2 = [p.sb(f"s_b2{i}", [128, H], F32) for i in range(2)]
    for i in range(NT):
        ab, bb = s_a[i % 2], s_b2[i % 2]
        p.dma("gpsimd", ab[:, :], o_x2p[i * 128:(i + 1) * 128, :], reads=[o_x2p], writes=[ab])
        p.dma("gpsimd", bb[:, :], shr_hbm[i * 128:(i + 1) * 128, :], reads=[shr_hbm], writes=[bb])
        p.op("vector", "tensor_add", ab[:, :], ab[:, :], bb[:, :],
             reads=[ab, bb], writes=[ab])
        p.dma("sync", o_x2p[i * 128:(i + 1) * 128, :], ab[:, :], reads=[ab], writes=[o_x2p])

    p.free(s_b2[1], s_b2[0], s_a[1], s_a[0])
    p.free(psS[1], psS[0])
    p.free(s_shT, s_sg, s_act[3], s_act[2], s_act[1], s_act[0])
    for k in list(cols):
        pass
    cl = list(cols.values()); cl.reverse()
    p.free(*cl)
    p.free(s_scr32, s_io32, s_ei, s_gates, s_gv, s_pun, s_l1tb, s_l1tf,
           s_mt, s_ln1b, s_ln1, s_sq2, s_x1, s_x0[1], s_x0[0])
    p.fence("gpsimd")
    p.fence("sync")
    s_tkf = p.sb("s_tkf", [32, 2048], F32)
    s_me = p.sb("s_me", [32, 2048], F32)
    s_on = p.sb("s_on", [32, 2048], F32)
    s_cum = p.sb("s_cum", [32, 2048], F32)
    s_carry = p.sb("s_carry", [32, 1], F32)
    s_io32c = p.sb("s_io32c", [32, 1], F32)
    s_rk = p.sb("s_rk", [1, 2048], F32)
    p.op("gpsimd", "memset", s_on[:, :], 1.0, writes=[s_on])
    p.op("gpsimd", "memset", s_carry[:, :], 0.0, writes=[s_carry])
    p.op("gpsimd", "iota", s_io32c[:, :], [[1, 1]], base=0, channel_multiplier=1,
         allow_small_or_imprecise_dtypes=True, writes=[s_io32c])
    for ch in range(4):
        p.dma("gpsimd", s_tkf[:, :],
              bass.AP(tke_hbm.t, ch * 2048, [[0, 32], [1, 2048]]),
              reads=[tke_hbm], writes=[s_tkf])
        p.op("vector", "tensor_scalar", s_me[:, :], s_tkf[:, :],
             s_io32c[:, :], None, ALU.is_equal,
             reads=[s_tkf, s_io32c], writes=[s_me])
        p.op("vector", "tensor_tensor_scan", s_cum[:, :], s_on[:, :], s_me[:, :],
             s_carry[:, :], ALU.mult, ALU.add,
             reads=[s_on, s_me, s_carry], writes=[s_cum], same_engine_raw=True)
        p.op("vector", "tensor_copy", s_carry[:, :], s_cum[:, 2047:2048],
             reads=[s_cum], writes=[s_carry])
        p.op("vector", "tensor_scalar_add", s_cum[:, :], s_cum[:, :], -1.0,
             reads=[s_cum], writes=[s_cum])
        p.op("vector", "tensor_mul", s_me[:, :], s_me[:, :], s_cum[:, :],
             reads=[s_me, s_cum], writes=[s_me])
        for q4 in range(4):
            p.op("tensor", "matmul", psR[:, :],
                 s_on[:, 0:1], s_me[:, q4 * 512:(q4 + 1) * 512],
                 start=True, stop=True, reads=[s_on, s_me], writes=[psR])
            p.op("vector", "tensor_copy", s_rk[:, q4 * 512:(q4 + 1) * 512], psR[:, :],
                 reads=[psR], writes=[s_rk])
        p.dma("sync", o_rank[:, ch * 2048:(ch + 1) * 2048], s_rk[:, :],
              reads=[s_rk], writes=[o_rank])
    p.dma("sync", o_counts[:, :], s_carry[:, :], reads=[s_carry], writes=[o_counts])

    return p.finalize()


def run_C(inputs, x_np, y, a_out):
    w_out = np.ascontiguousarray(inputs["L0_attn_w_out"], dtype=np.float32)
    gwT = np.ascontiguousarray(np.asarray(inputs["L0_ffn_gate_w"], np.float32).T)
    sgu = np.ascontiguousarray(inputs["L0_ffn_shared_w_gate_up"], dtype=np.float32)
    sdw = np.ascontiguousarray(inputs["L0_ffn_shared_w_down"], dtype=np.float32)
    dpar = np.ascontiguousarray(
        np.asarray(inputs["L0_attn_D_param"], np.float32).reshape(16, 128).T)
    in_maps = []
    for c in range(NCORES):
        t0 = c * TPC
        in_maps.append(dict(
            x0=x_np[t0:t0 + TPC],
            yin=np.ascontiguousarray(y[:, t0:t0 + TPC]),
            xssm=np.ascontiguousarray(a_out["xssm"][:, t0:t0 + TPC]),
            zin=np.ascontiguousarray(a_out["z"][:, t0:t0 + TPC]),
            dpar=dpar, w_out=w_out, gwT=gwT, sgu=sgu, sdw=sdw))
    nc = _get_nc("C", build_C)
    res = run_bass_kernel_spmd(nc, in_maps, core_ids=list(range(NCORES)))
    r = res.results
    out = {}
    out["x2p"] = np.concatenate([r[c]["o_x2p"] for c in range(NCORES)], axis=0)
    out["ln1"] = np.concatenate([r[c]["o_ln1"] for c in range(NCORES)], axis=0)
    out["tke"] = np.concatenate([r[c]["o_tke"] for c in range(NCORES)], axis=0)
    out["gates"] = np.concatenate([r[c]["o_gates"] for c in range(NCORES)], axis=0)
    out["rank"] = np.concatenate([r[c]["o_rank"][0] for c in range(NCORES)], axis=0)
    out["counts"] = np.stack([r[c]["o_counts"][:, 0] for c in range(NCORES)], axis=0)
    if DEBUG_C:
        out["x1"] = np.concatenate([r[c]["o_x1dbg"] for c in range(NCORES)], axis=0)
        out["mam"] = [r[c]["o_mamdbg"] for c in range(NCORES)]
    return out




def emit_pos_chunk(p, bufs, ch):
    s_tkb, s_me, s_pos, s_rank = bufs["tkb"], bufs["me"], bufs["pos"], bufs["rank"]
    psr = bufs["psr"]
    p.op("vector", "tensor_scalar", s_me[:, :], s_tkb[:, :], bufs["io32c"][:, :], None,
         ALU.is_equal, reads=[s_tkb, bufs["io32c"]], writes=[s_me])
    p.op("vector", "tensor_scalar", s_me[:, :], s_me[:, :], bufs["pref"][:, :], None,
         ALU.mult, reads=[s_me, bufs["pref"]], writes=[s_me])
    for q in range(4):
        p.op("tensor", "matmul", psr[:, :], bufs["one32"][:, :],
             s_me[:, q * 512:(q + 1) * 512], start=True, stop=True,
             reads=[bufs["one32"], s_me], writes=[psr])
        p.op("vector", "tensor_add", s_pos[:, q * 512:(q + 1) * 512],
             s_rank[:, q * 512:(q + 1) * 512], psr[:, :],
             reads=[s_rank, psr], writes=[s_pos])


def emit_prefix(p, counts8, myid, bufs):
    p.dma("gpsimd", bufs["c8"][:, :], counts8[:, :], reads=[counts8], writes=[bufs["c8"]])
    p.dma("gpsimd", bufs["myb"][:, :], bcast_ap(myid, 8, 1), reads=[myid], writes=[bufs["myb"]])
    p.op("gpsimd", "iota", bufs["io8"][:, :], [[1, 1]], base=0, channel_multiplier=1,
         allow_small_or_imprecise_dtypes=True, writes=[bufs["io8"]])
    p.op("gpsimd", "iota", bufs["io32c"][:, :], [[1, 1]], base=0, channel_multiplier=1,
         allow_small_or_imprecise_dtypes=True, writes=[bufs["io32c"]])
    p.op("gpsimd", "memset", bufs["one8"][:, :], 1.0, writes=[bufs["one8"]])
    p.op("gpsimd", "memset", bufs["one32"][:, :], 1.0, writes=[bufs["one32"]])
    p.op("vector", "tensor_scalar", bufs["cm"][:, :], bufs["io8"][:, :],
         bufs["myb"][:, :], None, ALU.subtract,
         reads=[bufs["io8"], bufs["myb"]], writes=[bufs["cm"]])
    p.op("vector", "tensor_single_scalar", bufs["cm"][:, :], bufs["cm"][:, :], -0.5,
         ALU.is_lt, reads=[bufs["cm"]], writes=[bufs["cm"]])
    p.op("vector", "tensor_scalar", bufs["msk"][:, :], bufs["c8"][:, :],
         bufs["cm"][:, :], None, ALU.mult,
         reads=[bufs["c8"], bufs["cm"]], writes=[bufs["msk"]])
    p.op("tensor", "matmul", bufs["ps1"][:, :], bufs["msk"][:, :], bufs["one8"][:, :],
         start=True, stop=True, reads=[bufs["msk"], bufs["one8"]], writes=[bufs["ps1"]])
    p.op("vector", "tensor_copy", bufs["pref"][:, :], bufs["ps1"][:, :],
         reads=[bufs["ps1"]], writes=[bufs["pref"]])


def mk_pos_bufs(p, pre):
    CH = 2048
    return dict(
        c8=p.sb(pre + "c8", [8, 32], F32), io8=p.sb(pre + "io8", [8, 1], F32),
        myb=p.sb(pre + "myb", [8, 1], F32), cm=p.sb(pre + "cm", [8, 1], F32),
        msk=p.sb(pre + "msk", [8, 32], F32), pref=p.sb(pre + "pref", [32, 1], F32),
        one8=p.sb(pre + "one8", [8, 1], F32), io32c=p.sb(pre + "io32c", [32, 1], F32),
        one32=p.sb(pre + "one32", [32, 1], F32),
        tkb=p.sb(pre + "tkb", [32, CH], F32), me=p.sb(pre + "me", [32, CH], F32),
        pos=p.sb(pre + "pos", [1, CH], F32), rank=p.sb(pre + "rank", [1, CH], F32),
        ps1=p.ps(pre + "ps1", [32, 1], F32), psr=p.ps(pre + "psr", [1, 512], F32))


def build_D():
    p = Prog("D")
    counts8 = p.dram("counts8", [8, 32], F32, kind="ExternalInput")
    tke_in = p.dram("tke_in", [TPC, 4], F32, kind="ExternalInput")
    gates_in = p.dram("gates_in", [TPC, 4], F32, kind="ExternalInput")
    rank_in = p.dram("rank_in", [1, 4 * TPC], F32, kind="ExternalInput")
    myid = p.dram("myid", [1, 1], F32, kind="ExternalInput")
    o_slot = p.dram("o_slot", [32 * 1024, 2], F32, kind="ExternalOutput")
    fld_hbm = p.dram("fld_hbm", [4, 4 * TPC], F32)

    NA = 4 * TPC
    CH = 2048
    bufs = mk_pos_bufs(p, "d_")
    emit_prefix(p, counts8, myid, bufs)

    s_t = p.sb("d_t", [1, CH], F32)
    s_posi = p.sb("d_posi", [1, CH], I32)
    s_m7 = p.sb("d_m7", [1, CH], I32)
    s_d7 = p.sb("d_d7", [1, CH], I32)
    s_g = p.sb("d_g", [1, CH], F32)
    s_ix = p.sb("d_ix", [1, CH], F32)
    s_tmp = p.sb("d_tmp", [1, CH], F32)
    s_tke1 = p.sb("d_tke1", [1, CH], F32)
    s_myb1 = p.sb("d_myb1", [1, 1], F32)
    s_my2k = p.sb("d_my2k", [1, 1], F32)
    p.dma("gpsimd", s_myb1[:, :], myid[:, :], reads=[myid], writes=[s_myb1])
    p.op("vector", "tensor_scalar_mul", s_my2k[:, :], s_myb1[:, :], float(TPC),
         reads=[s_myb1], writes=[s_my2k])

    for ch in range(4):
        a0 = ch * CH
        p.dma("gpsimd", bufs["tkb"][:, :],
              bass.AP(tke_in.t, a0, [[0, 32], [1, CH]]), reads=[tke_in], writes=[bufs["tkb"]])
        p.dma("gpsimd", bufs["rank"][:, :], rank_in[:, a0:a0 + CH],
              reads=[rank_in], writes=[bufs["rank"]])
        emit_pos_chunk(p, bufs, ch)
        p.dma("gpsimd", s_g[:, :], bass.AP(gates_in.t, a0, [[NA, 1], [1, CH]]),
              reads=[gates_in], writes=[s_g])
        p.dma("gpsimd", s_tke1[:, :], bass.AP(tke_in.t, a0, [[NA, 1], [1, CH]]),
              reads=[tke_in], writes=[s_tke1])
        p.op("gpsimd", "iota", s_t[:, :], [[1, CH // 4], [0, 4]],
             base=1 + a0 // 4, channel_multiplier=0,
             allow_small_or_imprecise_dtypes=True, writes=[s_t])
        p.op("vector", "tensor_scalar", s_t[:, :], s_t[:, :], s_my2k[:, :], None,
             ALU.add, reads=[s_t, s_my2k], writes=[s_t])
        pos = bufs["pos"]
        p.op("vector", "tensor_copy", s_posi[:, :], pos[:, :], reads=[pos], writes=[s_posi])
        p.op("vector", "tensor_single_scalar", s_m7[:, :], s_posi[:, :], 127,
             ALU.bitwise_and, reads=[s_posi], writes=[s_m7])
        p.op("vector", "tensor_single_scalar", s_d7[:, :], s_posi[:, :], 7,
             ALU.logical_shift_right, reads=[s_posi], writes=[s_d7])
        p.op("vector", "tensor_copy", s_tmp[:, :], s_m7[:, :], reads=[s_m7], writes=[s_tmp])
        p.op("vector", "tensor_copy", s_ix[:, :], s_d7[:, :], reads=[s_d7], writes=[s_ix])
        p.op("vector", "scalar_tensor_tensor", s_ix[:, :], s_tmp[:, :], 8.0, s_ix[:, :],
             ALU.mult, ALU.add, reads=[s_tmp, s_ix], writes=[s_ix])
        p.op("vector", "scalar_tensor_tensor", s_ix[:, :], s_tke1[:, :], 1024.0, s_ix[:, :],
             ALU.mult, ALU.add, reads=[s_tke1, s_ix], writes=[s_ix])
        p.op("vector", "tensor_single_scalar", s_tmp[:, :], pos[:, :], 1024.0, ALU.is_ge,
             reads=[pos], writes=[s_tmp])
        p.op("vector", "scalar_tensor_tensor", s_ix[:, :], s_tmp[:, :], 65536.0, s_ix[:, :],
             ALU.mult, ALU.add, reads=[s_tmp, s_ix], writes=[s_ix])
        p.dma("sync", fld_hbm[0:1, a0:a0 + CH], s_t[:, :], reads=[s_t], writes=[fld_hbm])
        p.dma("sync", fld_hbm[1:2, a0:a0 + CH], s_g[:, :], reads=[s_g], writes=[fld_hbm])
        p.dma("sync", fld_hbm[2:3, a0:a0 + CH], s_ix[:, :], reads=[s_ix], writes=[fld_hbm])

    s_z = p.sb("d_z", [64, 1024], F32)
    p.op("gpsimd", "memset", s_z[:, :], 0.0, writes=[s_z])
    p.dma("sync", bass.AP(o_slot.t, 0, [[1024, 64], [1, 1024]]), s_z[:, :],
          reads=[s_z], writes=[o_slot])

    s_f3 = p.sb("d_f3", [4, CH], F32)
    s_ft = p.sb("d_ft", [128, 4], F32)
    s_fi = p.sb("d_fi", [128, 1], I32)
    s_id4 = p.sb("d_id4", [4, 4], F32)
    ps_f = p.ps("d_psf", [128, 4], F32)
    p.op("gpsimd", "memset", s_id4[:, :], 1.0, writes=[s_id4])
    p.op("gpsimd", "affine_select", s_id4[:, :], s_id4[:, :], [[1, 4]],
         ALU.is_equal, 0.0, base=0, channel_multiplier=-1, reads=[s_id4], writes=[s_id4])
    for ch in range(4):
        a0 = ch * CH
        p.dma("gpsimd", s_f3[0:3, :], fld_hbm[0:3, a0:a0 + CH],
              reads=[fld_hbm], writes=[s_f3])
        for ti in range(CH // 128):
            p.op("tensor", "transpose", ps_f[:, 0:4],
                 s_f3[0:4, ti * 128:(ti + 1) * 128], s_id4[:, :],
                 reads=[s_f3, s_id4], writes=[ps_f])
            p.op("vector", "tensor_copy", s_ft[:, :], ps_f[:, 0:4],
                 reads=[ps_f], writes=[s_ft])
            p.op("vector", "tensor_copy", s_fi[:, :], s_ft[:, 2:3],
                 reads=[s_ft], writes=[s_fi])
            p.idma("gpsimd", o_slot.h, s_ft[:, 0:2],
                   bass.IndirectOffsetOnAxis(s_fi[:, :], 0), None,
                   bounds_check=32767, oob_is_err=False,
                   reads=[s_ft, s_fi], writes=[o_slot])
    return p.finalize()



def run_D(inputs, c_out):
    counts8 = np.ascontiguousarray(c_out["counts"].astype(np.float32))
    in_maps = []
    for c in range(NCORES):
        in_maps.append(dict(
            counts8=counts8,
            tke_in=np.ascontiguousarray(c_out["tke"][c * TPC:(c + 1) * TPC]),
            gates_in=np.ascontiguousarray(c_out["gates"][c * TPC:(c + 1) * TPC]),
            rank_in=np.ascontiguousarray(c_out["rank"][c * 4 * TPC:(c + 1) * 4 * TPC][None, :]),
            myid=np.array([[c]], np.float32)))
    nc = _get_nc("D", build_D)
    res = run_bass_kernel_spmd(nc, in_maps, core_ids=list(range(NCORES)))
    return np.stack([res.results[c]["o_slot"] for c in range(NCORES)], 0)



def build_E():
    p = Prog("E")
    slots = [p.dram(f"slot{c}", [4 * 1024, 2], F32, kind="ExternalInput")
             for c in range(NCORES)]
    ln1f = p.dram("ln1f", [NTOK, H], BF16, kind="ExternalInput")
    wgu = p.dram("wgu", [4 * H, H], F32, kind="ExternalInput")
    wdn = p.dram("wdn", [4 * (H // 2), H], F32, kind="ExternalInput")
    o_eo = p.dram("o_eo", [4 * 1024, H], BF16, kind="ExternalOutput")

    s_mrg = p.sb("e_mrg", [128, 16], F32)
    s_ld = p.sb("e_ld", [128, 16], F32)
    s_idx = p.sb("e_idx", [128, 8], I32)
    s_tmp8 = p.sb("e_tmp8", [128, 8], F32)
    s_buf = [p.sb(f"e_buf{g}", [128, H], BF16) for g in range(8)]
    s_bufT = p.sb("e_bufT", [128, 8 * 1024], BF16)
    s_wgu = [p.sb(f"e_wgu{c}", [128, H], BF16) for c in range(8)]
    s_wdn = [p.sb(f"e_wdn{c}", [128, H], BF16) for c in range(4)]
    s_id = p.sb("e_id", [128, 128], BF16)
    s_sg = p.sb("e_sg", [128, 512], F32)
    s_act = [p.sb(f"e_act{i}", [128, 1024], BF16) for i in range(4)]
    s_eoT = p.sb("e_eoT", [128, 8 * 1024], BF16)
    s_eo = p.sb("e_eo", [128, H], BF16)
    ps_g = [p.ps(f"e_psg{i}", [128, 512], F32) for i in range(2)]
    ps_t = p.ps("e_pst", [128, 128], BF16)
    p.op("gpsimd", "memset", s_id[:, :], 1.0, writes=[s_id])
    p.op("gpsimd", "affine_select", s_id[:, :], s_id[:, :], [[1, 128]],
         ALU.is_equal, 0.0, base=0, channel_multiplier=-1, reads=[s_id], writes=[s_id])

    for ei in range(4):
        for c in range(8):
            src = bass.AP(slots[c].t, ei * 1024 * 2, [[16, 128], [1, 16]])
            if c == 0:
                p.dma("gpsimd", s_mrg[:, :], src, reads=[slots[c]], writes=[s_mrg])
            else:
                p.dma("gpsimd", s_ld[:, :], src, reads=[slots[c]], writes=[s_ld])
                p.op("vector", "tensor_add", s_mrg[:, :], s_mrg[:, :], s_ld[:, :],
                     reads=[s_mrg, s_ld], writes=[s_mrg])
        p.op("vector", "tensor_copy", s_tmp8[:, :],
             bass.AP(s_mrg.h, 0, [[16, 128], [2, 8]]), reads=[s_mrg], writes=[s_tmp8])
        p.op("vector", "tensor_scalar_add", s_tmp8[:, :], s_tmp8[:, :], -1.0,
             reads=[s_tmp8], writes=[s_tmp8])
        p.op("vector", "tensor_single_scalar", s_idx[:, :], s_tmp8[:, :], 0.0,
             ALU.max, reads=[s_tmp8], writes=[s_idx]) if False else None
        p.op("vector", "tensor_single_scalar", s_tmp8[:, :], s_tmp8[:, :], -0.5,
             ALU.is_lt, reads=[s_tmp8], writes=[s_tmp8]) if False else None
        p.op("vector", "tensor_copy", s_idx[:, :], s_tmp8[:, :],
             reads=[s_tmp8], writes=[s_idx]) if False else None
        s_m8 = s_ld
        p.op("vector", "tensor_single_scalar", s_m8[:, 0:8], s_tmp8[:, :], -0.5,
             ALU.is_lt, reads=[s_tmp8], writes=[s_m8])
        p.op("vector", "scalar_tensor_tensor", s_tmp8[:, :], s_m8[:, 0:8], 65536.0,
             s_tmp8[:, :], ALU.mult, ALU.add, reads=[s_m8, s_tmp8], writes=[s_tmp8])
        p.op("vector", "tensor_copy", s_idx[:, :], s_tmp8[:, :],
             reads=[s_tmp8], writes=[s_idx])
        for c in range(8):
            p.dma("gpsimd", s_wgu[c][:, :], wgu[ei * H + c * 128:ei * H + (c + 1) * 128, :],
                  reads=[wgu], writes=[s_wgu[c]])
        for c in range(4):
            p.dma("gpsimd", s_wdn[c][:, :], wdn[ei * 512 + c * 128:ei * 512 + (c + 1) * 128, :],
                  reads=[wdn], writes=[s_wdn[c]])
        for g in range(8):
            p.op("vector", "memset", s_buf[g][:, :], 0.0, writes=[s_buf[g]])
            p.idma("gpsimd", s_buf[g][:, :], ln1f.h,
                   None, bass.IndirectOffsetOnAxis(s_idx[:, g:g + 1], 0),
                   bounds_check=NTOK - 1, oob_is_err=False,
                   reads=[ln1f, s_idx], writes=[s_buf[g]])
            for c in range(8):
                p.op("tensor", "transpose", ps_t[:, :],
                     s_buf[g][:, c * 128:(c + 1) * 128], s_id[:, :],
                     reads=[s_buf[g], s_id], writes=[ps_t])
                p.op("vector", "tensor_copy",
                     s_bufT[:, c * 1024 + g * 128:c * 1024 + (g + 1) * 128],
                     ps_t[:, :], reads=[ps_t], writes=[s_bufT])
        for oc in range(8):
            for sl in range(2):
                pst = ps_g[sl]
                for c in range(8):
                    p.op("tensor", "matmul", pst[:, :],
                         s_wgu[c][:, oc * 128:(oc + 1) * 128],
                         s_bufT[:, c * 1024 + sl * 512:c * 1024 + (sl + 1) * 512],
                         start=(c == 0), stop=(c == 7),
                         reads=[s_wgu[c], s_bufT], writes=[pst])
                if oc < 4:
                    p.op("scalar", "activation", s_sg[:, :], pst[:, :], AF.Silu,
                         reads=[pst], writes=[s_sg])
                    p.op("vector", "tensor_copy", s_act[oc][:, sl * 512:(sl + 1) * 512],
                         s_sg[:, :], reads=[s_sg], writes=[s_act[oc]])
                else:
                    p.op("vector", "tensor_mul", s_act[oc - 4][:, sl * 512:(sl + 1) * 512],
                         s_act[oc - 4][:, sl * 512:(sl + 1) * 512], pst[:, :],
                         reads=[s_act[oc - 4], pst], writes=[s_act[oc - 4]])
        for oc in range(8):
            for sl in range(2):
                pst = ps_g[sl]
                for ic in range(4):
                    p.op("tensor", "matmul", pst[:, :],
                         s_wdn[ic][:, oc * 128:(oc + 1) * 128],
                         s_act[ic][:, sl * 512:(sl + 1) * 512],
                         start=(ic == 0), stop=(ic == 3),
                         reads=[s_wdn[ic], s_act[ic]], writes=[pst])
                p.op("scalar", "copy",
                     s_eoT[:, oc * 1024 + sl * 512:oc * 1024 + (sl + 1) * 512],
                     pst[:, :], reads=[pst], writes=[s_eoT])
        for g in range(8):
            for oc in range(8):
                p.op("tensor", "transpose", ps_t[:, :],
                     s_eoT[:, oc * 1024 + g * 128:oc * 1024 + (g + 1) * 128],
                     s_id[:, :], reads=[s_eoT, s_id], writes=[ps_t])
                p.op("vector", "tensor_copy", s_eo[:, oc * 128:(oc + 1) * 128],
                     ps_t[:, :], reads=[ps_t], writes=[s_eo])
            p.op("vector", "tensor_scalar", s_eo[:, :], s_eo[:, :],
                 bass.AP(s_mrg.h, 2 * g + 1, [[16, 128], [1, 1]]), None, ALU.mult,
                 reads=[s_eo, s_mrg], writes=[s_eo])
            p.dma("sync", o_eo[ei * 1024 + g * 128:ei * 1024 + (g + 1) * 128, :],
                  s_eo[:, :], reads=[s_eo], writes=[o_eo])
    return p.finalize()


def run_E(inputs, slot8, ln1):
    wgu_full = np.asarray(inputs["L0_ffn_w_gate_up"], np.float32)
    wdn_full = np.asarray(inputs["L0_ffn_w_down"], np.float32)
    in_maps = []
    for c in range(NCORES):
        sl = {f"slot{k}": np.ascontiguousarray(
            slot8[k][c * 4 * 1024:(c + 1) * 4 * 1024]) for k in range(NCORES)}
        in_maps.append(dict(
            ln1f=ln1,
            wgu=np.ascontiguousarray(wgu_full[c * 4:(c + 1) * 4].reshape(4 * H, H)),
            wdn=np.ascontiguousarray(wdn_full[c * 4:(c + 1) * 4].reshape(4 * 512, H)),
            **sl))
    nc = _get_nc("E", build_E)
    res = run_bass_kernel_spmd(nc, in_maps, core_ids=list(range(NCORES)))
    return np.concatenate([res.results[c]["o_eo"] for c in range(NCORES)], axis=0)



def build_F():
    p = Prog("F")
    counts8 = p.dram("counts8", [8, 32], F32, kind="ExternalInput")
    tke_in = p.dram("tke_in", [TPC, 4], F32, kind="ExternalInput")
    rank_in = p.dram("rank_in", [1, 4 * TPC], F32, kind="ExternalInput")
    myid = p.dram("myid", [1, 1], F32, kind="ExternalInput")
    x2p_in = p.dram("x2p_in", [TPC, H], F32, kind="ExternalInput")
    eo_in = p.dram("eo_in", [32 * 1024, H], BF16, kind="ExternalInput")
    wqkT = p.dram("wqkT", [H, 2 * H], F32, kind="ExternalInput")
    wvT = p.dram("wvT", [H, H], F32, kind="ExternalInput")
    bqkv = p.dram("bqkv", [3 * H, 1], F32, kind="ExternalInput")
    o_x2 = p.dram("o_x2", [TPC, H], F32, kind="ExternalOutput")
    o_qkT = p.dram("o_qkT", [2 * H, TPC], BF16, kind="ExternalOutput")
    o_vtok = p.dram("o_vtok", [TPC, H], BF16, kind="ExternalOutput")
    ln2_hbm = p.dram("ln2_hbm", [TPC, H], BF16)
    fldf_hbm = p.dram("fldf_hbm", [1, 4 * TPC], F32)

    CH = 2048
    bufs = mk_pos_bufs(p, "f_")
    emit_prefix(p, counts8, myid, bufs)

    s_ix = p.sb("f_ix", [1, CH], F32)
    s_tmp = p.sb("f_tmp", [1, CH], F32)
    s_tke1 = p.sb("f_tke1", [1, CH], F32)
    for ch in range(4):
        a0 = ch * CH
        p.dma("gpsimd", bufs["tkb"][:, :], bass.AP(tke_in.t, a0, [[0, 32], [1, CH]]),
              reads=[tke_in], writes=[bufs["tkb"]])
        p.dma("gpsimd", bufs["rank"][:, :], rank_in[:, a0:a0 + CH],
              reads=[rank_in], writes=[bufs["rank"]])
        emit_pos_chunk(p, bufs, ch)
        p.dma("gpsimd", s_tke1[:, :], bass.AP(tke_in.t, a0, [[4 * TPC, 1], [1, CH]]),
              reads=[tke_in], writes=[s_tke1])
        pos = bufs["pos"]
        p.op("vector", "scalar_tensor_tensor", s_ix[:, :], s_tke1[:, :], 1024.0,
             pos[:, :], ALU.mult, ALU.add, reads=[s_tke1, pos], writes=[s_ix])
        p.op("vector", "tensor_single_scalar", s_tmp[:, :], pos[:, :], 1024.0,
             ALU.is_ge, reads=[pos], writes=[s_tmp])
        p.op("vector", "scalar_tensor_tensor", s_ix[:, :], s_tmp[:, :], 65536.0,
             s_ix[:, :], ALU.mult, ALU.add, reads=[s_tmp, s_ix], writes=[s_ix])
        p.dma("sync", fldf_hbm[0:1, a0:a0 + CH], s_ix[:, :], reads=[s_ix], writes=[fldf_hbm])

    s_sel4 = p.sb("f_sel4", [128, 512], BF16)
    p.op("gpsimd", "memset", s_sel4[:, :], 1.0, writes=[s_sel4])
    ap3 = bass.AP(s_sel4.h, 0, [[512, 128], [128, 4], [1, 128]])
    p.op("gpsimd", "affine_select", ap3, ap3, [[128, 4], [-4, 128]],
         ALU.is_ge, 0.0, base=0, channel_multiplier=1, reads=[s_sel4], writes=[s_sel4])
    p.op("gpsimd", "affine_select", ap3, ap3, [[-128, 4], [4, 128]],
         ALU.is_ge, 0.0, base=3, channel_multiplier=-1, reads=[s_sel4], writes=[s_sel4])

    s_if = p.sb("f_if", [4, CH], F32)
    s_ift = p.sb("f_ift", [128, 1], F32)
    s_ifi = p.sb("f_ifi", [128, 1], I32)
    s_id4 = p.sb("f_id4", [4, 4], F32)
    s_go = [p.sb(f"f_go{i}", [128, H], BF16) for i in range(2)]
    s_x2t = p.sb("f_x2t", [128, H], F32)
    s_rt = p.sb("f_rt", [128, H], F32)
    ps_f = p.ps("f_psf", [128, 4], F32)
    ps_r = [p.ps(f"f_psr{i}", [128, 512], F32) for i in range(2)]
    p.op("gpsimd", "memset", s_id4[:, :], 1.0, writes=[s_id4])
    p.op("gpsimd", "affine_select", s_id4[:, :], s_id4[:, :], [[1, 4]],
         ALU.is_equal, 0.0, base=0, channel_multiplier=-1, reads=[s_id4], writes=[s_id4])

    s_sq3 = p.sb("f_sq3", [128, H], F32)
    s_ln2 = p.sb("f_ln2", [128, H], BF16)
    s_ln2f = p.sb("f_ln2f", [128, H], F32)
    cols = {k: p.sb("f_" + k, [128, 1], F32)
            for k in ["sums", "sumsq", "mean", "ex2", "var", "d1", "sq1", "r1"]}

    for ch in range(4):
        a0 = ch * CH
        p.dma("gpsimd", s_if[0:1, :], fldf_hbm[0:1, a0:a0 + CH],
              reads=[fldf_hbm], writes=[s_if])
        for blk in range(4):
            tile128 = ch * 4 + blk
            for q in range(4):
                ti = blk * 4 + q
                p.op("tensor", "transpose", ps_f[:, 0:4],
                     s_if[0:4, ti * 128:(ti + 1) * 128], s_id4[:, :],
                     reads=[s_if, s_id4], writes=[ps_f])
                p.op("vector", "tensor_copy", s_ift[:, :], ps_f[:, 0:1],
                     reads=[ps_f], writes=[s_ift])
                p.op("vector", "tensor_copy", s_ifi[:, :], s_ift[:, :],
                     reads=[s_ift], writes=[s_ifi])
                gob = s_go[q % 2]
                p.op("vector", "memset", gob[:, :], 0.0, writes=[gob])
                p.idma("gpsimd", gob[:, :], eo_in.h,
                       None, bass.IndirectOffsetOnAxis(s_ifi[:, :], 0),
                       bounds_check=32 * 1024 - 1, oob_is_err=False,
                       reads=[eo_in, s_ifi], writes=[gob])
                for half in range(2):
                    p.op("tensor", "matmul", ps_r[half][:, :],
                         s_sel4[:, q * 128:(q + 1) * 128],
                         gob[:, half * 512:(half + 1) * 512],
                         start=(q == 0), stop=(q == 3),
                         reads=[s_sel4, gob], writes=[ps_r[half]])
            p.dma("gpsimd", s_x2t[:, :], x2p_in[tile128 * 128:(tile128 + 1) * 128, :],
                  reads=[x2p_in], writes=[s_x2t])
            p.op("vector", "tensor_add", s_rt[:, 0:512], s_x2t[:, 0:512], ps_r[0][:, :],
                 reads=[s_x2t, ps_r[0]], writes=[s_rt])
            p.op("vector", "tensor_add", s_rt[:, 512:1024], s_x2t[:, 512:1024],
                 ps_r[1][:, :], reads=[s_x2t, ps_r[1]], writes=[s_rt])
            p.dma("sync", o_x2[tile128 * 128:(tile128 + 1) * 128, :], s_rt[:, :],
                  reads=[s_rt], writes=[o_x2])
            p.op("scalar", "activation", s_sq3[:, :], s_rt[:, :], AF.Square,
                 accum_out=cols["sumsq"][:, :], reads=[s_rt], writes=[s_sq3, cols["sumsq"]])
            p.op("vector", "reduce_sum", cols["sums"][:, :], s_rt[:, :], AX.X,
                 reads=[s_rt], writes=[cols["sums"]])
            p.op("vector", "tensor_scalar_mul", cols["mean"][:, :], cols["sums"][:, :],
                 1.0 / H, reads=[cols["sums"]], writes=[cols["mean"]])
            p.op("vector", "tensor_scalar_mul", cols["ex2"][:, :], cols["sumsq"][:, :],
                 1.0 / H, reads=[cols["sumsq"]], writes=[cols["ex2"]])
            p.op("vector", "tensor_mul", cols["var"][:, :], cols["mean"][:, :],
                 cols["mean"][:, :], reads=[cols["mean"]], writes=[cols["var"]])
            p.op("vector", "tensor_sub", cols["var"][:, :], cols["ex2"][:, :],
                 cols["var"][:, :], reads=[cols["ex2"], cols["var"]], writes=[cols["var"]])
            p.op("vector", "tensor_scalar_add", cols["d1"][:, :], cols["var"][:, :],
                 1e-5, reads=[cols["var"]], writes=[cols["d1"]])
            p.op("scalar", "sqrt", cols["sq1"][:, :], cols["d1"][:, :],
                 reads=[cols["d1"]], writes=[cols["sq1"]])
            p.op("vector", "reciprocal", cols["r1"][:, :], cols["sq1"][:, :],
                 reads=[cols["sq1"]], writes=[cols["r1"]])
            p.op("vector", "tensor_scalar", s_ln2[:, :], s_rt[:, :], cols["mean"][:, :],
                 cols["r1"][:, :], ALU.subtract, ALU.mult,
                 reads=[s_rt, cols["mean"], cols["r1"]], writes=[s_ln2])
            p.dma("sync", ln2_hbm[tile128 * 128:(tile128 + 1) * 128, :], s_ln2[:, :],
                 reads=[s_ln2], writes=[ln2_hbm])

    s_l2T = [p.sb(f"f_l2T{c}", [128, TPC], BF16) for c in range(8)]
    s_wqk = [p.sb(f"f_wqk{c}", [128, 2 * H], BF16) for c in range(8)]
    s_wv = [p.sb(f"f_wv{c}", [128, H], BF16) for c in range(8)]
    s_bq = p.sb("f_bq", [128, 24], F32)
    s_qo = p.sb("f_qo", [128, 512], BF16)
    s_vp = p.sb("f_vp", [128, H], BF16)
    s_vbb = p.sb("f_vbb", [128, H], BF16)
    ps_q = [p.ps(f"f_psq{i}", [128, 512], F32) for i in range(2)]
    p.dma("gpsimd", s_vbb[:, :], bass.AP(bqkv.t, 2 * H, [[0, 128], [1, H]]),
          reads=[bqkv], writes=[s_vbb])
    for c in range(8):
        p.dma("sync", s_l2T[c][:, :], ln2_hbm[:, c * 128:(c + 1) * 128],
              transpose=True, reads=[ln2_hbm], writes=[s_l2T[c]])
        p.dma("gpsimd", s_wqk[c][:, :], wqkT[c * 128:(c + 1) * 128, :],
              reads=[wqkT], writes=[s_wqk[c]])
        p.dma("gpsimd", s_wv[c][:, :], wvT[c * 128:(c + 1) * 128, :],
              reads=[wvT], writes=[s_wv[c]])
    p.dma("gpsimd", s_bq[:, :], bass.AP(bqkv.t, 0, [[1, 128], [128, 24]]),
          reads=[bqkv], writes=[s_bq], allow_slow_non_contiguous=True)
    for oc in range(16):
        for ti in range(4):
            pst = ps_q[ti % 2]
            for c in range(8):
                p.op("tensor", "matmul", pst[:, :],
                     s_wqk[c][:, oc * 128:(oc + 1) * 128],
                     s_l2T[c][:, ti * 512:(ti + 1) * 512],
                     start=(c == 0), stop=(c == 7),
                     reads=[s_wqk[c], s_l2T[c]], writes=[pst])
            p.op("vector", "tensor_scalar", s_qo[:, :], pst[:, :],
                 s_bq[:, oc:oc + 1], None, ALU.add,
                 reads=[pst, s_bq], writes=[s_qo])
            p.dma("sync", o_qkT[oc * 128:(oc + 1) * 128, ti * 512:(ti + 1) * 512],
                  s_qo[:, :], reads=[s_qo], writes=[o_qkT])
    for ti2 in range(16):
        for oc in range(2):
            pst = ps_q[oc % 2]
            for c in range(8):
                p.op("tensor", "matmul", pst[:, :],
                     s_l2T[c][:, ti2 * 128:(ti2 + 1) * 128],
                     s_wv[c][:, oc * 512:(oc + 1) * 512],
                     start=(c == 0), stop=(c == 7),
                     reads=[s_l2T[c], s_wv[c]], writes=[pst])
            p.op("vector", "tensor_copy", s_vp[:, oc * 512:(oc + 1) * 512], pst[:, :],
                 reads=[pst], writes=[s_vp])
        p.op("vector", "tensor_add", s_vp[:, :], s_vp[:, :], s_vbb[:, :],
             reads=[s_vp, s_vbb], writes=[s_vp])
        p.dma("sync", o_vtok[ti2 * 128:(ti2 + 1) * 128, :], s_vp[:, :],
              reads=[s_vp], writes=[o_vtok])
    return p.finalize()



def run_F(inputs, c_out, eo_full):
    W = np.asarray(inputs["L1_attn_W"], np.float32)
    WT = np.ascontiguousarray(W.T)
    counts8 = np.ascontiguousarray(c_out["counts"].astype(np.float32))
    in_maps = []
    for c in range(NCORES):
        t0 = c * TPC
        in_maps.append(dict(
            counts8=counts8,
            tke_in=np.ascontiguousarray(c_out["tke"][t0:t0 + TPC]),
            rank_in=np.ascontiguousarray(c_out["rank"][c * 4 * TPC:(c + 1) * 4 * TPC][None, :]),
            myid=np.array([[c]], np.float32),
            x2p_in=np.ascontiguousarray(c_out["x2p"][t0:t0 + TPC]),
            eo_in=eo_full,
            wqkT=np.ascontiguousarray(WT[:, 0:2 * H]),
            wvT=np.ascontiguousarray(WT[:, 2 * H:]),
            bqkv=np.ascontiguousarray(np.asarray(inputs["L1_attn_b"], np.float32)[:, None])))
    nc = _get_nc("F", build_F)
    res = run_bass_kernel_spmd(nc, in_maps, core_ids=list(range(NCORES)))
    r = res.results
    out = {}
    out["x2"] = np.concatenate([r[c]["o_x2"] for c in range(NCORES)], axis=0)
    out["qkT"] = np.concatenate([r[c]["o_qkT"] for c in range(NCORES)], axis=1)
    out["vtok"] = np.concatenate([r[c]["o_vtok"] for c in range(NCORES)], axis=0)
    return out




def build_G():
    p = Prog("G")
    qt_in = p.dram("qt_in", [4, 64, T], BF16, kind="ExternalInput")
    kt_in = p.dram("kt_in", [4, 64, T], BF16, kind="ExternalInput")
    v_in = p.dram("v_in", [4, T, 64], BF16, kind="ExternalInput")
    o_ot = p.dram("o_ot", [4, 65, T], F32, kind="ExternalOutput")

    NKT = T // 128
    NQB = T // 512
    s_qt = p.sb("g_qt", [64, T], BF16)
    s_kt = p.sb("g_kt", [64, T], BF16)
    s_v = p.sb("g_v", [128, 65 * NKT], BF16)
    s_p = [p.sb(f"g_p{i}", [128, 512], BF16) for i in range(2)]
    s_msk = p.sb("g_msk", [128, 4 * 512], BF16)
    s_o = p.sb("g_o", [128, 512], F32)
    s_nb = p.sb("g_nb", [128, 1], F32)
    ps_s = [p.ps(f"g_pss{i}", [128, 512], F32) for i in range(2)]
    ps_o = p.ps("g_pso", [128, 512], F32)
    p.op("gpsimd", "memset", s_nb[:, :], -4.0, writes=[s_nb])

    p.op("gpsimd", "memset", s_msk[:, :], 1.0, writes=[s_msk])
    mv = bass.AP(s_msk.h, 0, [[4 * 512, 128], [512, 4], [1, 512]])
    p.op("gpsimd", "affine_select", mv, mv, [[-128, 4], [1, 512]],
         ALU.is_ge, 0.0, base=0, channel_multiplier=-1, reads=[s_msk], writes=[s_msk])

    for pr in range(4):
        p.dma("gpsimd", s_qt[:, :], qt_in[pr, :, :], reads=[qt_in], writes=[s_qt])
        p.dma("gpsimd", s_kt[:, :], kt_in[pr, :, :], reads=[kt_in], writes=[s_kt])
        p.op("gpsimd", "memset", s_v[:, :], 1.0, writes=[s_v])
        for kt in range(NKT):
            p.dma("gpsimd", s_v[:, kt * 65:kt * 65 + 64],
                  v_in[pr, kt * 128:(kt + 1) * 128, :], reads=[v_in], writes=[s_v])
        for qb in range(NQB):
            nkt = 4 * qb + 4
            for kt in range(nkt):
                pss = ps_s[kt % 2]
                pb = s_p[kt % 2]
                p.op("tensor", "matmul", pss[:, :],
                     s_kt[:, kt * 128:(kt + 1) * 128],
                     s_qt[:, qb * 512:(qb + 1) * 512],
                     start=True, stop=True, reads=[s_kt, s_qt], writes=[pss])
                p.op("scalar", "activation", pb[:, :], pss[:, :], AF.Exp,
                     scale=0.125, bias=s_nb[:, :], reads=[pss, s_nb], writes=[pb])
                d = kt - 4 * qb
                if d >= 0:
                    p.op("vector", "tensor_mul", pb[:, :], pb[:, :],
                         s_msk[:, d * 512:(d + 1) * 512], reads=[pb, s_msk], writes=[pb])
                p.op("tensor", "matmul", ps_o[0:65, :],
                     s_v[:, kt * 65:(kt + 1) * 65], pb[:, :],
                     start=(kt == 0), stop=(kt == nkt - 1),
                     reads=[s_v, pb], writes=[ps_o])
            p.op("vector", "tensor_copy", s_o[0:65, :], ps_o[0:65, :],
                 reads=[ps_o], writes=[s_o])
            p.dma("sync", o_ot[pr, :, qb * 512:(qb + 1) * 512], s_o[0:65, :],
                  reads=[s_o], writes=[o_ot])
    return p.finalize()


def run_G(inputs, f_out):
    qkT = f_out["qkT"]
    vtok = f_out["vtok"]
    in_maps = []
    for c in range(NCORES):
        b = c // 4
        hs = (c % 4) * 4
        qs, ks, vs = [], [], []
        for j in range(4):
            h = hs + j
            qs.append(qkT[h * 64:(h + 1) * 64, b * T:(b + 1) * T])
            ks.append(qkT[1024 + h * 64:1024 + (h + 1) * 64, b * T:(b + 1) * T])
            vs.append(vtok[b * T:(b + 1) * T, h * 64:(h + 1) * 64])
        in_maps.append(dict(
            qt_in=np.ascontiguousarray(np.stack(qs, 0)),
            kt_in=np.ascontiguousarray(np.stack(ks, 0)),
            v_in=np.ascontiguousarray(np.stack(vs, 0))))
    nc = _get_nc("G", build_G)
    res = run_bass_kernel_spmd(nc, in_maps, core_ids=list(range(NCORES)))
    ot = np.zeros((2, 16, 65, T), np.float32)
    for c in range(NCORES):
        b = c // 4
        hs = (c % 4) * 4
        ot[b, hs:hs + 4] = res.results[c]["o_ot"]
    return ot



def build_H():
    p = Prog("H")
    x2_in = p.dram("x2_in", [TPC, H], F32, kind="ExternalInput")
    o_in = p.dram("o_in", [16, 65, TPC], F32, kind="ExternalInput")
    wqT = p.dram("wqT", [H, 2 * H], F32, kind="ExternalInput")
    keysT = p.dram("keysT", [16 * 128, 256], F32, kind="ExternalInput")
    nrmw = p.dram("nrmw", [1, H], F32, kind="ExternalInput")
    o_x3 = p.dram("o_x3", [TPC, H], F32, kind="ExternalOutput")
    o_xn = p.dram("o_xn", [TPC, H], BF16, kind="ExternalOutput")
    o_fi = p.dram("o_fi", [TPC, 64], F32, kind="ExternalOutput")
    o_rfs = p.dram("o_rfs", [TPC, 64], F32, kind="ExternalOutput")
    xn_hbm = p.dram("xn_hbm", [TPC, H], BF16)

    NT = TPC // 128
    s_x2 = [p.sb(f"h_x2{i}", [128, H], F32) for i in range(2)]
    s_oc = p.sb("h_oc", [65, 128], F32)
    s_x3 = p.sb("h_x3", [128, H], F32)
    s_sq = p.sb("h_sq", [128, H], F32)
    s_xn = p.sb("h_xn", [128, H], BF16)
    s_xnf = p.sb("h_xnf", [128, H], F32)
    s_nw = p.sb("h_nw", [128, H], F32)
    s_id = p.sb("h_id", [128, 128], F32)
    cols = {k: p.sb("h_" + k, [128, 1], F32)
            for k in ["sums", "sumsq", "mean", "ex2", "var", "d1", "sq1", "r1",
                      "ratio", "d2", "sq2", "r2", "r12", "rcp"]}
    ps_t = p.ps("h_pst", [128, 128], F32)
    p.dma("gpsimd", s_nw[:, :], bcast_ap(nrmw, 128, H), reads=[nrmw], writes=[s_nw])
    p.op("gpsimd", "memset", s_id[:, :], 1.0, writes=[s_id])
    p.op("gpsimd", "affine_select", s_id[:, :], s_id[:, :], [[1, 128]],
         ALU.is_equal, 0.0, base=0, channel_multiplier=-1, reads=[s_id], writes=[s_id])

    for i in range(NT):
        xb = s_x2[i % 2]
        p.dma("gpsimd", xb[:, :], x2_in[i * 128:(i + 1) * 128, :], reads=[x2_in], writes=[xb])
        for h in range(16):
            p.op("tensor", "transpose", ps_t[:, 0:65],
                 o_in[h, :, i * 128:(i + 1) * 128], s_id[:, :],
                 reads=[o_in, s_id], writes=[ps_t]) if False else None
            p.dma("gpsimd", s_oc[0:65, 0:128],
                  o_in[h, :, i * 128:(i + 1) * 128], reads=[o_in], writes=[s_oc]) \
                if False else None
            pass
        for h in range(16):
            p.dma("gpsimd", s_oc[0:65, 0:128], o_in[h, :, i * 128:(i + 1) * 128],
                  reads=[o_in], writes=[s_oc])
            p.op("tensor", "transpose", ps_t[:, 0:65], s_oc[0:65, 0:128], s_id[0:65, 0:65],
                 reads=[s_oc, s_id], writes=[ps_t])
            p.op("vector", "tensor_copy", s_x3[:, h * 64:h * 64 + 64], ps_t[:, 0:64],
                 reads=[ps_t], writes=[s_x3])
            p.op("vector", "reciprocal", cols["rcp"][:, :], ps_t[:, 64:65],
                 reads=[ps_t], writes=[cols["rcp"]])
            p.op("vector", "tensor_scalar", s_x3[:, h * 64:h * 64 + 64],
                 s_x3[:, h * 64:h * 64 + 64], cols["rcp"][:, :], None, ALU.mult,
                 reads=[s_x3, cols["rcp"]], writes=[s_x3])
        p.op("vector", "tensor_add", s_x3[:, :], s_x3[:, :], xb[:, :],
             reads=[s_x3, xb], writes=[s_x3])
        p.dma("sync", o_x3[i * 128:(i + 1) * 128, :], s_x3[:, :],
              reads=[s_x3], writes=[o_x3])
        p.op("scalar", "activation", s_sq[:, :], s_x3[:, :], AF.Square,
             accum_out=cols["sumsq"][:, :], reads=[s_x3], writes=[s_sq, cols["sumsq"]])
        p.op("vector", "reduce_sum", cols["sums"][:, :], s_x3[:, :], AX.X,
             reads=[s_x3], writes=[cols["sums"]])
        p.op("vector", "tensor_scalar_mul", cols["mean"][:, :], cols["sums"][:, :], 1.0 / H,
             reads=[cols["sums"]], writes=[cols["mean"]])
        p.op("vector", "tensor_scalar_mul", cols["ex2"][:, :], cols["sumsq"][:, :], 1.0 / H,
             reads=[cols["sumsq"]], writes=[cols["ex2"]])
        p.op("vector", "tensor_mul", cols["var"][:, :], cols["mean"][:, :], cols["mean"][:, :],
             reads=[cols["mean"]], writes=[cols["var"]])
        p.op("vector", "tensor_sub", cols["var"][:, :], cols["ex2"][:, :], cols["var"][:, :],
             reads=[cols["ex2"], cols["var"]], writes=[cols["var"]])
        p.op("vector", "tensor_scalar_add", cols["d1"][:, :], cols["var"][:, :], 1e-5,
             reads=[cols["var"]], writes=[cols["d1"]])
        p.op("vector", "reciprocal", cols["ratio"][:, :], cols["d1"][:, :],
             reads=[cols["d1"]], writes=[cols["ratio"]])
        p.op("vector", "tensor_mul", cols["ratio"][:, :], cols["var"][:, :], cols["ratio"][:, :],
             reads=[cols["var"], cols["ratio"]], writes=[cols["ratio"]])
        p.op("vector", "tensor_scalar_add", cols["d2"][:, :], cols["ratio"][:, :], 1e-6,
             reads=[cols["ratio"]], writes=[cols["d2"]])
        p.op("scalar", "sqrt", cols["sq1"][:, :], cols["d1"][:, :],
             reads=[cols["d1"]], writes=[cols["sq1"]])
        p.op("scalar", "sqrt", cols["sq2"][:, :], cols["d2"][:, :],
             reads=[cols["d2"]], writes=[cols["sq2"]])
        p.op("vector", "reciprocal", cols["r1"][:, :], cols["sq1"][:, :],
             reads=[cols["sq1"]], writes=[cols["r1"]])
        p.op("vector", "reciprocal", cols["r2"][:, :], cols["sq2"][:, :],
             reads=[cols["sq2"]], writes=[cols["r2"]])
        p.op("vector", "tensor_mul", cols["r12"][:, :], cols["r1"][:, :], cols["r2"][:, :],
             reads=[cols["r1"], cols["r2"]], writes=[cols["r12"]])
        p.op("vector", "tensor_scalar", s_xnf[:, :], s_x3[:, :], cols["mean"][:, :],
             cols["r12"][:, :], ALU.subtract, ALU.mult,
             reads=[s_x3, cols["mean"], cols["r12"]], writes=[s_xnf])
        p.op("vector", "tensor_mul", s_xn[:, :], s_xnf[:, :], s_nw[:, :],
             reads=[s_xnf, s_nw], writes=[s_xn])
        p.dma("sync", o_xn[i * 128:(i + 1) * 128, :], s_xn[:, :],
              reads=[s_xn], writes=[o_xn])
        p.dma("sync", xn_hbm[i * 128:(i + 1) * 128, :], s_xn[:, :],
              reads=[s_xn], writes=[xn_hbm])

    s_xT = [p.sb(f"h_xT{c}", [128, TPC], BF16) for c in range(8)]
    s_wq = [p.sb(f"h_wq{c}", [128, 2 * H], BF16) for c in range(8)]
    s_keys = p.sb("h_keys", [128, 16 * 256], BF16)
    s_qc = p.sb("h_qc", [128, TPC], BF16)
    s_sim = p.sb("h_sim", [128, 256], F32)
    s_s1 = p.sb("h_s1", [128, 8], F32)
    s_s2 = p.sb("h_s2", [128, 8], F32)
    s_i1 = p.sb("h_i1", [128, 8], F32)
    s_i2 = p.sb("h_i2", [128, 8], F32)
    s_io256 = p.sb("h_io256", [128, 256], F32)
    s_scr = p.sb("h_scr", [128, 256], F32)
    s_comb = p.sb("h_comb", [128, 64], F32)
    s_cfi = p.sb("h_cfi", [128, 64], F32)
    s_fs = p.sb("h_fs", [128, 8], F32)
    s_fi8 = p.sb("h_fi8", [128, 64], F32)
    s_rfs = p.sb("h_rfs", [128, 64], F32)
    ps_q = [p.ps(f"h_psq{i}", [128, 512], F32) for i in range(2)]
    ps_sim = p.ps("h_psim", [128, 256], F32)

    p.op("gpsimd", "iota", s_io256[:, :], [[1, 256]], base=0, channel_multiplier=0,
         allow_small_or_imprecise_dtypes=True, writes=[s_io256])
    for c in range(8):
        p.dma("sync", s_xT[c][:, :], xn_hbm[:, c * 128:(c + 1) * 128],
              transpose=True, reads=[xn_hbm], writes=[s_xT[c]])
        p.dma("gpsimd", s_wq[c][:, :], wqT[c * 128:(c + 1) * 128, :],
              reads=[wqT], writes=[s_wq[c]])
    for hc in range(16):
        p.dma("gpsimd", s_keys[:, hc * 256:(hc + 1) * 256],
              keysT[hc * 128:(hc + 1) * 128, :], reads=[keysT], writes=[s_keys])

    s_qc2 = p.sb("h_qc2", [128, TPC], BF16)
    for h in range(8):
        for sel in range(2):
            oc = h * 2 + sel
            qcb = s_qc if sel == 0 else s_qc2
            for ti in range(4):
                pst = ps_q[ti % 2]
                for c in range(8):
                    p.op("tensor", "matmul", pst[:, :],
                         s_wq[c][:, oc * 128:(oc + 1) * 128],
                         s_xT[c][:, ti * 512:(ti + 1) * 512],
                         start=(c == 0), stop=(c == 7),
                         reads=[s_wq[c], s_xT[c]], writes=[pst])
                p.op("scalar", "copy", qcb[:, ti * 512:(ti + 1) * 512], pst[:, :],
                     reads=[pst], writes=[qcb])
        for t in range(NT):
            for sel in range(2):
                oc = h * 2 + sel
                qcb = s_qc if sel == 0 else s_qc2
                p.op("tensor", "matmul", ps_sim[:, :],
                     qcb[:, t * 128:(t + 1) * 128],
                     s_keys[:, oc * 256:(oc + 1) * 256],
                     start=True, stop=True, reads=[qcb, s_keys], writes=[ps_sim])
                p.op("vector", "tensor_copy", s_sim[:, :], ps_sim[:, :],
                     reads=[ps_sim], writes=[s_sim])
                sv, iv = (s_s1, s_i1) if sel == 0 else (s_s2, s_i2)
                p.op("vector", "max", sv[:, :], s_sim[:, :], reads=[s_sim], writes=[sv])
                for j in range(8):
                    p.op("vector", "scalar_tensor_tensor", s_scr[:, :], s_sim[:, :],
                         sv[:, j:j + 1], s_io256[:, :], ALU.is_equal, ALU.mult,
                         accum_out=iv[:, j:j + 1],
                         reads=[s_sim, sv, s_io256], writes=[s_scr, iv])
            a1 = bass.AP(s_s1.h, 0, [[8, 128], [1, 8], [0, 8]])
            a2 = bass.AP(s_s2.h, 0, [[8, 128], [0, 8], [1, 8]])
            p.op("vector", "tensor_tensor",
                 bass.AP(s_comb.h, 0, [[64, 128], [8, 8], [1, 8]]), a1, a2,
                 ALU.add, reads=[s_s1, s_s2], writes=[s_comb])
            i1r = bass.AP(s_i1.h, 0, [[8, 128], [1, 8], [0, 8]])
            i2r = bass.AP(s_i2.h, 0, [[8, 128], [0, 8], [1, 8]])
            p.op("vector", "scalar_tensor_tensor",
                 bass.AP(s_cfi.h, 0, [[64, 128], [8, 8], [1, 8]]), i1r, 256.0, i2r,
                 ALU.mult, ALU.add, reads=[s_i1, s_i2], writes=[s_cfi])
            p.op("vector", "max", s_fs[:, :], s_comb[:, :], reads=[s_comb], writes=[s_fs])
            for j in range(8):
                p.op("vector", "scalar_tensor_tensor", s_scr[:, 0:64], s_comb[:, :],
                     s_fs[:, j:j + 1], s_cfi[:, :], ALU.is_equal, ALU.mult,
                     accum_out=s_fi8[:, h * 8 + j:h * 8 + j + 1],
                     reads=[s_comb, s_fs, s_cfi], writes=[s_scr, s_fi8])
            p.op("vector", "tensor_single_scalar", s_rfs[:, h * 8:(h + 1) * 8],
                 s_fs[:, :], 0.0, ALU.max, reads=[s_fs], writes=[s_rfs])
            p.dma("sync", o_fi[t * 128:(t + 1) * 128, h * 8:(h + 1) * 8],
                  s_fi8[:, h * 8:(h + 1) * 8], reads=[s_fi8], writes=[o_fi])
            p.dma("sync", o_rfs[t * 128:(t + 1) * 128, h * 8:(h + 1) * 8],
                  s_rfs[:, h * 8:(h + 1) * 8], reads=[s_rfs], writes=[o_rfs])
    return p.finalize()


def run_H(inputs, f_out, ot):
    w_q = np.asarray(inputs["L1_ffn_w_q"], np.float32)
    wqT = np.ascontiguousarray(w_q.T)
    keys = np.asarray(inputs["L1_ffn_keys"], np.float32)
    keysT = np.ascontiguousarray(keys.transpose(0, 2, 3, 1).reshape(16 * 128, 256))
    nrmw = np.asarray(inputs["L1_ffn_norm_w"], np.float32).reshape(1, H)
    in_maps = []
    for c in range(NCORES):
        b = c // 4
        t0 = (c % 4) * TPC
        in_maps.append(dict(
            x2_in=np.ascontiguousarray(f_out["x2"][c * TPC:(c + 1) * TPC]),
            o_in=np.ascontiguousarray(ot[b, :, :, t0:t0 + TPC]),
            wqT=wqT, keysT=keysT, nrmw=nrmw))
    nc = _get_nc("H", build_H)
    res = run_bass_kernel_spmd(nc, in_maps, core_ids=list(range(NCORES)))
    r = res.results
    out = {}
    out["x3"] = np.concatenate([r[c]["o_x3"] for c in range(NCORES)], axis=0)
    out["xn"] = np.concatenate([r[c]["o_xn"] for c in range(NCORES)], axis=0)
    out["fi"] = np.concatenate([r[c]["o_fi"] for c in range(NCORES)], axis=0)
    out["rfs"] = np.concatenate([r[c]["o_rfs"] for c in range(NCORES)], axis=0)
    return out



def build_I():
    p = Prog("I")
    tbl = p.dram("tbl", [65536, 2 * H], BF16, kind="ExternalInput")
    fi_in = p.dram("fi_in", [TPC, 64], F32, kind="ExternalInput")
    rfs_in = p.dram("rfs_in", [TPC, 64], F32, kind="ExternalInput")
    xn_in = p.dram("xn_in", [TPC, H], BF16, kind="ExternalInput")
    x3_in = p.dram("x3_in", [TPC, H], F32, kind="ExternalInput")
    o_x4 = p.dram("o_x4", [TPC, H], F32, kind="ExternalOutput")

    s_xg = p.sb("i_xg", [128, H], BF16)
    s_fif = p.sb("i_fif", [128, 64], F32)
    s_fii = p.sb("i_fii", [128, 64], I32)
    s_rfs = p.sb("i_rfs", [128, 64], F32)
    s_gat = [p.sb(f"i_gat{i}", [128, 2 * H], BF16) for i in range(6)]
    s_dot = [p.sb(f"i_dot{i}", [128, 1], F32) for i in range(2)]
    s_hid = [p.sb(f"i_hid{i}", [128, 1], F32) for i in range(2)]
    s_scr = [p.sb(f"i_scr{i}", [128, H], BF16) for i in range(2)]
    s_swu = [p.sb(f"i_swu{i}", [128, H], BF16) for i in range(2)]
    s_x3 = p.sb("i_x3", [128, H], F32)
    s_x4 = p.sb("i_x4", [128, H], F32)
    s_id = p.sb("i_id", [128, 128], BF16)
    ps_o = [p.ps(f"i_pso{i}", [128, 512], F32) for i in range(2)]
    p.op("gpsimd", "memset", s_id[:, :], 1.0, writes=[s_id])
    p.op("gpsimd", "affine_select", s_id[:, :], s_id[:, :], [[1, 128]],
         ALU.is_equal, 0.0, base=0, channel_multiplier=-1, reads=[s_id], writes=[s_id])

    for g in range(16):
        rows = bass.AP(xn_in.t, g * H, [[16 * H, 128], [1, H]])
        p.dma("gpsimd", s_xg[:, :], rows, reads=[xn_in], writes=[s_xg])
        p.dma("gpsimd", s_fif[:, :], bass.AP(fi_in.t, g * 64, [[16 * 64, 128], [1, 64]]),
              reads=[fi_in], writes=[s_fif])
        p.dma("gpsimd", s_rfs[:, :], bass.AP(rfs_in.t, g * 64, [[16 * 64, 128], [1, 64]]),
              reads=[rfs_in], writes=[s_rfs])
        p.op("vector", "tensor_copy", s_fii[:, :], s_fif[:, :],
             reads=[s_fif], writes=[s_fii])
        for il in range(64):
            gb = s_gat[il % 6]
            scr, dot, hid, swu = (s_scr[il % 2], s_dot[il % 2],
                                  s_hid[il % 2], s_swu[il % 2])
            p.idma("gpsimd", gb[:, :], tbl.h,
                   None, bass.IndirectOffsetOnAxis(s_fii[:, il:il + 1], 0),
                   reads=[tbl, s_fii], writes=[gb])
            p.op("vector", "scalar_tensor_tensor", scr[:, :], gb[:, 0:H], 1.0,
                 s_xg[:, :], ALU.mult, ALU.mult, accum_out=dot[:, :],
                 reads=[gb, s_xg], writes=[scr, dot])
            p.op("scalar", "activation", hid[:, :], dot[:, :], AF.Gelu,
                 reads=[dot], writes=[hid])
            p.op("vector", "tensor_mul", hid[:, :], hid[:, :], s_rfs[:, il:il + 1],
                 reads=[hid, s_rfs], writes=[hid])
            p.op("vector", "tensor_scalar", swu[:, :], gb[:, H:2 * H],
                 hid[:, :], None, ALU.mult, reads=[gb, hid], writes=[swu])
            for half in range(2):
                p.op("tensor", "matmul", ps_o[half][:, :], s_id[:, :],
                     swu[:, half * 512:(half + 1) * 512],
                     start=(il == 0), stop=(il == 63),
                     reads=[s_id, swu], writes=[ps_o[half]])
        p.dma("gpsimd", s_x3[:, :], bass.AP(x3_in.t, g * H, [[16 * H, 128], [1, H]]),
              reads=[x3_in], writes=[s_x3])
        for half in range(2):
            p.op("vector", "tensor_add", s_x4[:, half * 512:(half + 1) * 512],
                 s_x3[:, half * 512:(half + 1) * 512], ps_o[half][:, :],
                 reads=[s_x3, ps_o[half]], writes=[s_x4])
        p.dma("sync", bass.AP(o_x4.t, g * H, [[16 * H, 128], [1, H]]), s_x4[:, :],
              reads=[s_x4], writes=[o_x4])
    return p.finalize()


def run_I(inputs, h_out, tbl):
    in_maps = []
    for c in range(NCORES):
        t0 = c * TPC
        in_maps.append(dict(
            tbl=tbl,
            fi_in=np.ascontiguousarray(h_out["fi"][t0:t0 + TPC]),
            rfs_in=np.ascontiguousarray(h_out["rfs"][t0:t0 + TPC]),
            xn_in=np.ascontiguousarray(h_out["xn"][t0:t0 + TPC]),
            x3_in=np.ascontiguousarray(h_out["x3"][t0:t0 + TPC])))
    nc = _get_nc("I", build_I)
    res = run_bass_kernel_spmd(nc, in_maps, core_ids=list(range(NCORES)))
    return np.concatenate([res.results[c]["o_x4"] for c in range(NCORES)], axis=0)


def kernel(**inputs):
    x_np = np.asarray(inputs["L0_attn_x"], np.float32).reshape(NTOK, H)
    a_out = run_A(inputs)
    y = run_B(inputs, a_out)
    c_out = run_C(inputs, x_np, y, a_out)
    slot8 = run_D(inputs, c_out)
    eo = run_E(inputs, slot8, np.ascontiguousarray(c_out["ln1"]))
    f_out = run_F(inputs, c_out, eo)
    ot = run_G(inputs, f_out)
    h_out = run_H(inputs, f_out, ot)
    x4 = run_I(inputs, h_out, a_out["tbl"])
    return np.ascontiguousarray(x4.reshape(BATCH, T, H))
